# version 08
# baseline (speedup 1.0000x reference)
import numpy as np
import ml_dtypes
import concourse.bass as bass
import concourse.mybir as mybir
from concourse.bass_utils import run_bass_kernel_spmd

F32 = mybir.dt.float32
BF16 = mybir.dt.bfloat16
AF = mybir.ActivationFunctionType
ALU = mybir.AluOpType
AX = mybir.AxisListType

NCORE = 8
TMAIN = 1024
TH = 1028
NIC = 16
NEXP = 16
CH = 256
NCHUNK = 4
NH = 64
HP = 64
DIN = 4096
EPS = 1e-5
SBUF_BASE = 16512
SBUF_TOP = 229344

FFN1_GRPS = [(0, 257), (257, 257), (514, 257), (771, 257)]
FFN2_GRPS = [(3, 512), (515, 512)]


class Sem:
    def __init__(self, nc, name):
        self.h = nc.alloc_semaphore(name)
        self.n = 0


class P:
    def __init__(self, which, debug_stop=None, nexp=NEXP):
        self.which = which
        self.nexp = nexp
        self.debug_stop = debug_stop
        nc = bass.Bass("TRN2", target_bir_lowering=False)
        self.nc = nc
        self.q = {e: [] for e in ("tensor", "vector", "scalar", "gpsimd", "sync")}
        for s in ("sT", "sV", "sA", "sG", "sIO", "sWL0", "sWL1", "sWF", "sST"):
            setattr(self, s, Sem(nc, s))
        self.sFD = [Sem(nc, "sFD0"), Sem(nc, "sFD1")]
        self.sGC = Sem(nc, "sGC")
        self._sq_pend = None
        self._ssqt = [0, 0]
        self._mp_pairs = 0
        self._mp_war = [None, None]
        self.sb_off = SBUF_BASE
        self.wload_n = 0
        self._jit_free = 0
        self._jf2 = [0, 0]
        self.bankfree = [None] * 8

        self._declare_dram()
        self._declare_sbuf()
        self.pb = [nc.alloc_psum_tensor(f"pbank{i}", [128, 512], F32) for i in range(8)]
        self.pbf = [p.bitcast(BF16) for p in self.pb]

    def _declare_dram(self):
        nc = self.nc
        DI = lambda n, sh, dt=F32: nc.dram_tensor(n, sh, dt, kind="ExternalInput").ap()
        DO = lambda n, sh, dt=F32: nc.dram_tensor(n, sh, dt, kind="ExternalOutput").ap()
        d = {}
        if self.which == "L1":
            d["uT"] = DI("uT", [128, NIC * TH])
            d["x1out"] = DO("x1out", [128, NIC * TH])
            if self.debug_stop != "ffn1":
                d["Rout"] = DO("Rout", [128, DIN])
                d["cdout"] = DO("cdout", [64, 1])
            else:
                d["xnD"] = DO("xnD", [128, NIC * TH], BF16)
                d["gpackD"] = DO("gpackD", [48, TH])
                d["GbD"] = DO("GbD", [128, TH])
                d["agD"] = DO("agD", [128, NIC * TH], BF16)
        else:
            d["x1in"] = DI("x1in", [128, NIC * TH])
            d["prevR"] = DI("prevR", [7 * 128, DIN])
            d["prevcd"] = DI("prevcd", [7 * 64, 1])
            d["pmask"] = DI("pmask", [128, 7])
            d["oprojT"] = DI("oprojT", [4096, 2048])
            d["nwattn"] = DI("nwattn", [128, 32])
            d["dcol128"] = DI("dcol128", [128, 64])
            d["xout"] = DO("xout", [128, NIC * TMAIN])
        d["w1tp"] = DI("w1tp", [NEXP * 2048, 4096])
        d["w2t"] = DI("w2t", [NEXP * 2048, 2048])
        d["iprojT"] = DI("iprojT", [2048, 8512])
        d["gwt"] = DI("gwt", [128, NIC * 16])
        d["gateb"] = DI("gateb", [1, 16])
        d["b1sb"] = DI("b1sb", [128, 16 * 32])
        d["b2sb"] = DI("b2sb", [16, 2048])
        d["convw"] = DI("convw", [128, 34 * 4])
        d["convb"] = DI("convb", [128, 34])
        d["hsmall"] = DI("hsmall", [64, 3])
        d["nwffn"] = DI("nwffn", [128, NIC])
        d["mask0"] = DI("mask0", [128, CH])
        d["mask1"] = DI("mask1", [128, CH])
        d["eye_f"] = DI("eye_f", [128, 128])
        d["eye_bf"] = DI("eye_bf", [128, 128], BF16)
        d["halo_mask"] = DI("halo_mask", [128, 1])
        self.d = d

    def sb(self, name, shape, dtype, align=64):
        esz = 2 if dtype == BF16 else 4
        self.sb_off = (self.sb_off + align - 1) // align * align
        t = self.nc.alloc_sbuf_tensor_at(name, shape, dtype, offset=self.sb_off)
        self.sb_off += shape[1] * esz
        assert self.sb_off <= SBUF_TOP, f"sbuf overflow at {name}: {self.sb_off}"
        return t

    def _declare_sbuf(self):
        sb = self.sb
        self.X = sb("X", [128, NIC * TH], F32)
        self.wbuf = [sb(f"wbuf{i}", [128, 8192], BF16) for i in range(2)]
        self.scr1 = sb("scr1", [128, TH], F32)
        self.scr2 = sb("scr2", [128, TH], F32)
        self.Gb = sb("GbS", [128, TH], F32)
        self.Gtt = self.nc.alloc_sbuf_tensor_at(
            "GttS", [128, 9 * 16], F32,
            offset=self.sb_off - TH * 4)
        gbo = self.sb_off - TH * 4
        self.Mp2 = self.nc.alloc_sbuf_tensor_at(
            "Mp2S", [128, 2 * CH], BF16, offset=gbo + 1024)
        self.jitw2 = self.nc.alloc_sbuf_tensor_at(
            "jitw2S", [128, 192], BF16, offset=gbo + 2048)
        self.jitD2 = self.nc.alloc_sbuf_tensor_at(
            "jitD2S", [128, 128], BF16, offset=gbo + 2432)
        self.rows = sb("rows", [1, 2048], F32)
        self.onesrow = sb("onesrow", [1, 512], F32)
        self.gpack = sb("gpack", [128, TH + 2048], F32)
        self.logit = self.gpack[0:16, 0:TH]
        self.Gfm_a = self.gpack[32:48, 0:TH]
        self.Gfm_b = self.gpack[64:80, 0:TH]
        self.b2sb = self.gpack[64:80, TH:TH + 2048]
        self.ones_f = sb("ones_f", [1, 128], F32)
        self.onesc_f = sb("onesc_f", [128, 1], F32)
        self.epsc = sb("epsc", [1, 1], F32)
        self.dumc = sb("dumc", [1, 1], F32)
        self.eye_f = sb("eye_fS", [128, 128], F32)
        self.eye_bf = sb("eye_bfS", [128, 128], BF16)
        self.b1sb = sb("b1sbS", [128, 512], F32)
        self.gwt = sb("gwtS", [128, NIC * 16], BF16)
        self.gateb = sb("gatebS", [1, 16], F32)
        self.convw = sb("convwS", [128, 136], F32)
        self.convb = sb("convbS", [128, 34], F32)
        self.hsmall = sb("hsmallS", [64, 3], F32)
        self.negexpA = sb("negexpA", [64, 1], F32)
        self.nwffn = sb("nwffnS", [128, NIC], F32)
        self.mask0 = sb("mask0S", [128, CH], F32)
        self.mask1 = sb("mask1S", [128, CH], F32)
        self.halo_mask = sb("halo_maskS", [128, 1], F32)
        self.hsave = sb("hsave", [128, NIC * 3], F32)
        if self.which == "L2":
            self.nwattn = sb("nwattnS", [128, 32], F32)
            self.dcol128 = sb("dcol128S", [128, 64], F32)
            self.pmaskS = sb("pmaskSS", [128, 7], F32)
        phase = self.sb_off
        self.xn = sb("xn", [128, NIC * TH], BF16)
        self.ag = sb("ag", [128, NIC * TH], BF16)
        ffn_top = self.sb_off
        self.sb_off = phase
        self.xT = sb("xT", [128, 2 * DIN], BF16)
        self.xn2 = sb("xn2", [128, NIC * 259], BF16)
        self.prev = sb("prevS", [128, DIN], F32)
        mark = self.sb_off
        self.ynfm = sb("ynfm", [128, 32 * CH], BF16)
        ynfm_end = self.sb_off
        self.sb_off = mark
        self.pRscr = sb("pRscr", [128, DIN], F32)
        self.sb_off = max(self.sb_off, ynfm_end)
        self.Bfm = sb("BfmS", [128, CH], BF16)
        self.Cfm = sb("CfmS", [128, CH], BF16)
        self.Btok = sb("BtokS", [128, 2 * 128], BF16)
        self.dt_hm = sb("dt_hm", [64, CH], F32)
        self.dtv_hm = sb("dtv_hm", [64, CH], F32)
        self.dAc_hm = sb("dAc_hm", [64, CH], F32)
        self.tmp_hm = sb("tmp_hm", [64, CH], F32)
        self.dAcT = sb("dAcT", [128, 2 * 64], F32)
        self.dtT = sb("dtT", [128, 2 * 64], F32)
        self.dtvT = sb("dtvT", [128, 2 * 64], F32)
        self.cdc = sb("cdcS", [128, 64], F32)
        self.cd = sb("cdS", [64, 1], F32)
        self.cdp = sb("cdpS", [64, 1], F32)
        self.smallT = sb("smallTS", [64, 1], F32)
        self.Wp = sb("WpS", [128, 2 * CH], BF16)
        self.Mp = sb("MpS", [128, 2 * CH], BF16)
        self.convbf = self.nc.alloc_sbuf_tensor_at(
            "convbfS", [128, CH], BF16, offset=self.sb_off - 2 * CH * 2)
        self.jitw = sb("jitwS", [128, 192], BF16)
        self.jitD = sb("jitDS", [128, 128], BF16)
        self.Mpb = [self.Mp, self.Mp2]
        self.jitwb = [self.jitw, self.jitw2]
        self.jitDb = [self.jitD, self.jitD2]
        self.convs = sb("convsS", [128, 259], F32)
        self.convs2 = sb("convs2S", [128, 259], F32)
        self.ysc = sb("yscS", [128, CH], F32)
        mamba_top = self.sb_off
        assert max(ffn_top, mamba_top) <= SBUF_TOP, (ffn_top, mamba_top)

    def wait(self, q, sem, val):
        if val is None:
            return
        self.q[q].append(lambda eng, s=sem.h, v=val: eng.wait_ge(s, v))

    def op(self, q, build, inc=None, waits=()):
        if q == "gpsimd_c":
            q = "gpsimd"
            sem = self.sGC
            if sem.n > 0:
                self.wait(q, sem, sem.n)
            if inc is None:
                inc = (sem, 1)
        elif q in ("vector", "scalar"):
            sem = self.sV if q == "vector" else self.sA
            if sem.n > 0:
                self.wait(q, sem, sem.n)
            if inc is None:
                inc = (sem, 1)
        for w in waits:
            if w is not None:
                self.wait(q, w[0], w[1])
        tgt = None
        if inc is not None:
            inc[0].n += inc[1]
            tgt = inc[0].n

        def fn(eng, build=build, inc=inc):
            ins = build(eng)
            if inc is not None:
                ins.then_inc(inc[0].h, inc[1])

        self.q[q].append(fn)
        return tgt

    def guard(self, bank):
        g = self.bankfree[bank]
        return list(g) if g is not None else []

    def setg(self, bank, sem, val, extra=None):
        g = [(sem, val)]
        if extra is not None:
            g.append(extra)
        self.bankfree[bank] = g

    def wload(self, src):
        k = self.wload_n
        self.wload_n += 1
        buf = self.wbuf[k % 2]
        sem = self.sWL0 if k % 2 == 0 else self.sWL1
        a, b = src.shape[1], src.shape[2]
        dst = buf[:, 0:a * b].rearrange("p (a b) -> p a b", a=a)
        waits = []
        if k >= 2:
            waits.append((self.sWF, k - 1))
        tgt = self.op("gpsimd",
                      lambda eng, d=dst, s=src: eng.dma_start(out=d, in_=s),
                      inc=(sem, 16), waits=waits)
        return buf, (sem, tgt)

    def emit_setup(self):
        d, sIO = self.d, self.sIO
        loads = [
            ("gwt", self.gwt, "gpsimd"),
            ("eye_bf", self.eye_bf, "gpsimd"),
            ("b1sb", self.b1sb, "sync"), ("b2sb", self.b2sb, "sync"),
            ("gateb", self.gateb, "sync"), ("convw", self.convw, "sync"),
            ("convb", self.convb, "sync"), ("hsmall", self.hsmall, "sync"),
            ("nwffn", self.nwffn, "sync"),
            ("mask0", self.mask0, "sync"), ("mask1", self.mask1, "sync"),
            ("eye_f", self.eye_f, "sync"), ("halo_mask", self.halo_mask, "sync"),
        ]
        if self.which == "L2":
            loads += [("nwattn", self.nwattn, "sync"),
                      ("dcol128", self.dcol128, "sync"),
                      ("pmask", self.pmaskS, "sync")]
        for name, dst, qq in loads:
            self.op(qq, lambda eng, a=d[name], b=dst: eng.dma_start(out=b[:, :], in_=a),
                    inc=(sIO, 16) if qq == "sync" else (self.sG, 16))
        self.gload_ready = (self.sG, self.sG.n)
        src = d["uT"] if self.which == "L1" else d["x1in"]
        self.op("sync", lambda eng, a=src: eng.dma_start(out=self.X[:, :], in_=a),
                inc=(sIO, 16))
        self.x_ready = (sIO, sIO.n)
        self.consts = (sIO, sIO.n)
        self.op("vector", lambda eng: eng.memset(self.ones_f[:, :], 1.0))
        self.op("vector", lambda eng: eng.memset(self.onesc_f[:, :], 1.0))
        self.op("vector", lambda eng: eng.memset(self.onesrow[:, :], 1.0))
        self.op("vector", lambda eng: eng.memset(self.epsc[:, :], EPS))
        self.op("scalar", lambda eng: eng.activation(self.negexpA[:, :],
                                                     self.hsmall[:, 1:2], AF.Exp),
                waits=[self.consts], inc=(self.sA, 1))
        self.op("vector", lambda eng: eng.tensor_scalar(
            self.negexpA[:, :], self.negexpA[:, :], -1.0, None, ALU.mult),
            waits=[(self.sA, self.sA.n)], inc=(self.sV, 1))
        self.setup_done = (self.sV, self.sV.n)

    def emit_ffn(self, grps, x_ready, stages="all"):
        sT, sV, sA = self.sT, self.sV, self.sA
        pb = self.pb
        X, xn, ag = self.X, self.xn, self.ag
        first_t = [x_ready, self.setup_done]
        if self.sGC.n > 0:
            self.wait("vector", self.sGC, self.sGC.n)

        for (c0, w) in grps:
            sq_done = []
            for ic in range(NIC):
                slot = self.scr1[:, 0:w] if ic % 2 == 0 else self.scr1[:, 514:514 + w]
                wts = [x_ready] if ic < 2 else [(sT, sq_done[ic - 2][1])]
                a_t = self.op("scalar",
                              lambda eng, s=slot, i=ic, c0=c0, w=w: eng.activation(
                                  s, X[:, i * TH + c0:i * TH + c0 + w], AF.Square),
                              waits=wts + ([x_ready] if ic < 2 else []),
                              inc=(sA, 1))
                g6 = self.guard(6) if ic == 0 else []
                self.op("tensor",
                        lambda eng, i=ic, c0=c0, w=w: eng.matmul(
                            pb[6][0:1, 0:w], self.onesc_f[:, :],
                            X[:, i * TH + c0:i * TH + c0 + w],
                            start=(i == 0), stop=(i == NIC - 1)),
                        waits=(first_t if ic == 0 else []) + g6)
                g7 = self.guard(7) if ic == 0 else []
                t = self.op("tensor",
                            lambda eng, s=slot, i=ic, w=w: eng.matmul(
                                pb[7][0:1, 0:w], self.onesc_f[:, :], s,
                                start=(i == 0), stop=(i == NIC - 1)),
                            waits=[(sA, a_t)] + g7,
                            inc=(sT, 1))
                sq_done.append((a_t, t))
            R = self.rows
            mvw = [(sT, sq_done[-1][1])]
            self.op("vector", lambda eng, w=w: eng.tensor_scalar(
                R[0:1, 0:w], pb[6][0:1, 0:w], 1.0 / 2048, None, ALU.mult), waits=mvw)
            self.op("vector", lambda eng, w=w: eng.tensor_scalar(
                R[0:1, 1024:1024 + w], pb[7][0:1, 0:w], 1.0 / 2048, None, ALU.mult))
            self.op("vector", lambda eng, w=w: eng.tensor_tensor(
                R[0:1, 512:512 + w], R[0:1, 0:w], R[0:1, 0:w], ALU.mult))
            self.op("vector", lambda eng, w=w: eng.tensor_tensor(
                R[0:1, 512:512 + w], R[0:1, 1024:1024 + w], R[0:1, 512:512 + w],
                ALU.subtract))
            self.op("vector", lambda eng, w=w: eng.tensor_scalar(
                R[0:1, 1024:1024 + w], R[0:1, 512:512 + w], EPS, None, ALU.add))
            self.op("vector", lambda eng, w=w: eng.reciprocal(
                R[0:1, 1536:1536 + w], R[0:1, 1024:1024 + w]))
            v1 = self.op("vector", lambda eng, w=w: eng.tensor_tensor(
                R[0:1, 1536:1536 + w], R[0:1, 512:512 + w], R[0:1, 1536:1536 + w],
                ALU.mult), inc=(sV, 1))
            self.op("scalar", lambda eng, w=w: eng.activation(
                R[0:1, 1024:1024 + w], R[0:1, 1024:1024 + w], AF.Ln),
                waits=[(sV, v1)])
            a1 = self.op("scalar", lambda eng, w=w: eng.activation(
                R[0:1, 1536:1536 + w], R[0:1, 1536:1536 + w], AF.Ln,
                bias=self.epsc[0:1, 0:1]), inc=(sA, 1))
            v2 = self.op("vector", lambda eng, w=w: eng.tensor_tensor(
                R[0:1, 512:512 + w], R[0:1, 1024:1024 + w], R[0:1, 1536:1536 + w],
                ALU.add), waits=[(sA, a1)], inc=(sV, 1))
            a2 = self.op("scalar", lambda eng, w=w: eng.activation(
                R[0:1, 512:512 + w], R[0:1, 512:512 + w], AF.Exp, scale=-0.5),
                waits=[(sV, v2)], inc=(sA, 1))
            ab1 = self.op("scalar", lambda eng, w=w: eng.activation(
                R[0:1, 1536:1536 + w], R[0:1, 1024:1024 + w], AF.Exp, scale=-0.5),
                inc=(sA, 1))
            v3 = self.op("vector", lambda eng, w=w: eng.tensor_tensor(
                R[0:1, 1024:1024 + w], R[0:1, 0:w], R[0:1, 512:512 + w], ALU.mult),
                waits=[(sA, ab1)], inc=(sV, 1))
            self.op("tensor", lambda eng, w=w: eng.matmul(
                pb[6][:, 0:w], self.ones_f[:, :], R[0:1, 512:512 + w],
                start=True, stop=True), waits=[(sV, v3)])
            t2 = self.op("tensor", lambda eng, w=w: eng.matmul(
                pb[7][:, 0:w], self.ones_f[:, :], R[0:1, 1024:1024 + w],
                start=True, stop=True), inc=(sT, 1))
            self.op("vector", lambda eng, w=w: eng.tensor_scalar(
                R[0:1, 1536:1536 + w], R[0:1, 1536:1536 + w], 0.5, None, ALU.mult))
            self.op("vector", lambda eng, w=w: eng.tensor_tensor(
                R[0:1, 512:512 + w], R[0:1, 1536:1536 + w], R[0:1, 0:w], ALU.mult),
                waits=[(sT, t2)])
            self.op("vector", lambda eng, w=w: eng.tensor_scalar(
                R[0:1, 1536:1536 + w], R[0:1, 1536:1536 + w], 1.0, None, ALU.add),
                inc=(sV, 1))
            lastv = None
            for ic in range(NIC):
                self.op("vector", lambda eng, i=ic, c0=c0, w=w: eng.tensor_tensor(
                    self.scr2[:, 0:w], X[:, i * TH + c0:i * TH + c0 + w],
                    pb[6][:, 0:w], ALU.mult),
                    waits=[(sT, t2)] if ic == 0 else [])
                self.op("vector", lambda eng, i=ic, w=w: eng.tensor_tensor(
                    self.scr2[:, 514:514 + w], self.scr2[:, 0:w], pb[7][:, 0:w],
                    ALU.subtract))
                lastv = self.op("vector", lambda eng, i=ic, c0=c0, w=w: eng.tensor_scalar(
                    xn[:, i * TH + c0:i * TH + c0 + w], self.scr2[:, 514:514 + w],
                    self.nwffn[:, i:i + 1], None, ALU.mult), inc=(sV, 1))
            self.op("tensor", lambda eng, w=w: eng.matmul(
                pb[6][:, 0:w], self.ones_f[:, :], R[0:1, 1536:1536 + w],
                start=True, stop=True), waits=[(sV, lastv)])
            tb2 = self.op("tensor", lambda eng, w=w: eng.matmul(
                pb[7][:, 0:w], self.ones_f[:, :], R[0:1, 512:512 + w],
                start=True, stop=True), inc=(sT, 1))
            lastr = None
            for ic in range(NIC):
                self.op("vector", lambda eng, i=ic, c0=c0, w=w: eng.tensor_tensor(
                    self.scr2[:, 0:w], X[:, i * TH + c0:i * TH + c0 + w],
                    pb[6][:, 0:w], ALU.mult),
                    waits=[(sT, tb2)] if ic == 0 else [])
                lastr = self.op("vector", lambda eng, i=ic, c0=c0, w=w: eng.tensor_tensor(
                    X[:, i * TH + c0:i * TH + c0 + w], self.scr2[:, 0:w],
                    pb[7][:, 0:w], ALU.subtract), inc=(sV, 1))
            self.setg(6, sV, lastr)
            self.setg(7, sV, lastr)

        if stages == "ln":
            return (sV, sV.n)
        for (c0, w) in grps:
            for ic in range(NIC):
                self.op("tensor", lambda eng, i=ic, c0=c0, w=w: eng.matmul(
                    pb[6][0:16, 0:w], self.gwt[:, i * 16:(i + 1) * 16],
                    xn[:, i * TH + c0:i * TH + c0 + w],
                    start=(i == 0), stop=False),
                    waits=(self.guard(6) + [self.gload_ready]) if ic == 0 else [])
            tg = self.op("tensor", lambda eng, w=w: eng.matmul(
                pb[6][0:16, 0:w], self.gateb[:, :], self.onesrow[0:1, 0:w],
                start=False, stop=True), inc=(sT, 1))
            ve = self.op("vector", lambda eng, c0=c0, w=w: eng.tensor_copy(
                self.logit[:, c0:c0 + w], pb[6][0:16, 0:w]),
                waits=[(sT, tg)], inc=(sV, 1))
            self.setg(6, sV, ve)
            logit_done = ve
        lo = min(c0 for c0, w in grps)
        hi = max(c0 + w for c0, w in grps)
        tiles = []
        o = lo
        while o < hi:
            tiles.append((o, min(128, hi - o)))
            o += 128
        S2 = self.scr2
        for ti, (o, tw) in enumerate(tiles):
            tt = self.op("tensor", lambda eng, o=o, tw=tw: eng.transpose(
                pb[7][0:tw, 0:16], self.logit[:, o:o + tw],
                self.eye_f[0:16, 0:16]),
                waits=self.guard(7) + ([(sV, logit_done)] if ti == 0 else []),
                inc=(sT, 1))
            c = self.op("vector", lambda eng, tw=tw: eng.tensor_copy(
                S2[0:tw, 0:16], pb[7][0:tw, 0:16]), waits=[(sT, tt)], inc=(sV, 1))
            self.setg(7, sV, c)
            self.op("vector", lambda eng, tw=tw: eng.reduce_max(
                S2[0:tw, 16:17], S2[0:tw, 0:16], AX.X, negate=True))
            vx = self.op("vector", lambda eng, tw=tw: eng.memset(
                S2[0:tw, 48:64], 0.0), inc=(sV, 1))
            ae = self.op("scalar", lambda eng, tw=tw: eng.activation(
                S2[0:tw, 32:48], S2[0:tw, 0:16], AF.Exp, bias=S2[0:tw, 16:17]),
                waits=[(sV, vx)], inc=(sA, 1))
            for j in range(16):
                self.op("vector", lambda eng, j=j, tw=tw: eng.tensor_scalar(
                    S2[0:tw, 64:80], S2[0:tw, 0:16], S2[0:tw, j:j + 1], None,
                    ALU.is_lt))
                self.op("vector", lambda eng, tw=tw: eng.tensor_tensor(
                    S2[0:tw, 48:64], S2[0:tw, 48:64], S2[0:tw, 64:80], ALU.add))
            self.op("vector", lambda eng, tw=tw: eng.tensor_scalar(
                S2[0:tw, 48:64], S2[0:tw, 48:64], 4.0, None, ALU.is_lt))
            self.op("vector", lambda eng, tw=tw: eng.tensor_tensor(
                S2[0:tw, 48:64], S2[0:tw, 48:64], S2[0:tw, 32:48], ALU.mult),
                waits=[(sA, ae)])
            self.op("vector", lambda eng, tw=tw: eng.reduce_sum(
                S2[0:tw, 16:17], S2[0:tw, 48:64], AX.X))
            self.op("vector", lambda eng, tw=tw: eng.reciprocal(
                S2[0:tw, 17:18], S2[0:tw, 16:17]))
            gv = self.op("vector", lambda eng, ti=ti, tw=tw: eng.tensor_scalar(
                self.Gtt[0:tw, ti * 16:(ti + 1) * 16], S2[0:tw, 48:64],
                S2[0:tw, 17:18], None, ALU.mult), inc=(sV, 1))
            tb = self.op("tensor", lambda eng, ti=ti, o=o, tw=tw: eng.transpose(
                pb[7][0:16, 0:tw], self.Gtt[0:tw, ti * 16:(ti + 1) * 16],
                self.eye_f[0:tw, 0:tw]),
                waits=[(sV, gv)] + self.guard(7), inc=(sT, 1))
            c2 = self.op("vector", lambda eng, o=o, tw=tw: eng.tensor_scalar(
                self.Gfm_a[0:16, o:o + tw], pb[7][0:16, 0:tw],
                0.5 / 1.702, None, ALU.mult), waits=[(sT, tb)])
            c3 = self.op("vector", lambda eng, o=o, tw=tw: eng.tensor_scalar(
                self.Gfm_b[0:16, o:o + tw], pb[7][0:16, 0:tw],
                0.5, None, ALU.mult), inc=(sV, 1))
            self.setg(7, sV, c3)

        g_done = (sV, sV.n)
        if stages == "gate":
            return (sV, sV.n)
        for e in range(self.nexp):
            for gi, (c0, w) in enumerate(grps):
                gm = self.op("tensor", lambda eng, e=e, c0=c0, w=w: eng.matmul(
                    pb[6][0:1, 0:w], self.eye_f[32:48, 32 + e:33 + e],
                    self.Gfm_a[:, c0:c0 + w], start=True, stop=True),
                    waits=self.guard(6) + ([g_done] if (e == 0 and gi == 0) else []),
                    inc=(sT, 1))
                gv0 = self.op("vector", lambda eng, w=w: eng.tensor_copy(
                    self.rows[0:1, 1024:1024 + w], pb[6][0:1, 0:w]),
                    waits=[(sT, gm)], inc=(sV, 1))
                gm2 = self.op("tensor", lambda eng, w=w: eng.matmul(
                    pb[6][:, 0:w], self.ones_f[:, :], self.rows[0:1, 1024:1024 + w],
                    start=True, stop=True), waits=[(sV, gv0)], inc=(sT, 1))
                gv = self.op("vector", lambda eng, c0=c0, w=w: eng.tensor_copy(
                    self.Gb[:, c0:c0 + w], pb[6][:, 0:w]),
                    waits=[(sT, gm2)], inc=(sV, 1))
                self.setg(6, sV, gv)
            unit = 0
            for pr in range(NEXP):
                og = pr // 2
                if pr % 2 == 0:
                    wb, wrdy = self.wload(
                        self.d["w1tp"][e * 2048:(e + 1) * 2048,
                                       og * 512:(og + 1) * 512]
                        .rearrange("(ic p) o -> p ic o", p=128))
                for gi, (c0, w) in enumerate(grps):
                    bA, bB = unit % 2, 2 + unit % 2
                    wts = []
                    if gi == 0 and pr % 2 == 0:
                        wts.append(wrdy)
                    wts += self.guard(bA)
                    for ic in range(NIC):
                        self.op("tensor", lambda eng, ic=ic, pr=pr, c0=c0, w=w,
                                wb=wb, bA=bA: eng.matmul(
                                    pb[bA][:, 0:w],
                                    wb[:, ic * 512 + (pr % 2) * 256:
                                       ic * 512 + (pr % 2) * 256 + 128],
                                    xn[:, ic * TH + c0:ic * TH + c0 + w],
                                    start=(ic == 0), stop=(ic == NIC - 1)),
                                waits=wts if ic == 0 else [])
                    wts2 = self.guard(bB)
                    for ic in range(NIC):
                        tB = self.op("tensor", lambda eng, ic=ic, pr=pr, c0=c0, w=w,
                                     wb=wb, bB=bB: eng.matmul(
                                         pb[bB][:, 0:w],
                                         wb[:, ic * 512 + (pr % 2) * 256 + 128:
                                            ic * 512 + (pr % 2) * 256 + 256],
                                         xn[:, ic * TH + c0:ic * TH + c0 + w],
                                         start=(ic == 0), stop=(ic == NIC - 1)),
                                     waits=wts2 if ic == 0 else [],
                                     inc=(sT, 1) if ic == NIC - 1 else None)
                    last_w1 = (pr % 2 == 1 and gi == len(grps) - 1)
                    b1a = self.b1sb[:, e * 32 + 2 * pr:e * 32 + 2 * pr + 1]
                    b1b = self.b1sb[:, e * 32 + 2 * pr + 1:e * 32 + 2 * pr + 2]
                    self.op("vector", lambda eng, bA=bA, w=w, b1a=b1a: eng.tensor_scalar(
                        self.scr1[:, 0:w], pb[bA][:, 0:w], b1a, 7.0, ALU.add, ALU.min),
                        waits=[(sT, tB)])
                    if last_w1:
                        self._dummy_wfree()
                    uv = self.op("vector", lambda eng, bB=bB, w=w, b1b=b1b: eng.tensor_scalar(
                        self.scr2[:, 0:w], pb[bB][:, 0:w], b1b, 7.0, ALU.add, ALU.min),
                        inc=(sV, 1))
                    self.setg(bA, sV, uv)
                    self.setg(bB, sV, uv)
                    self.op("vector", lambda eng, w=w: eng.tensor_scalar(
                        self.scr2[:, 0:w], self.scr2[:, 0:w], -7.0, 1.0,
                        ALU.max, ALU.add))
                    at = self.op("scalar", lambda eng, w=w: eng.activation(
                        self.scr1[:, 514:514 + w], self.scr1[:, 0:w], AF.Silu,
                        scale=1.702), waits=[(sV, uv)], inc=(sA, 1))
                    self.op("vector", lambda eng, w=w: eng.tensor_tensor(
                        self.scr2[:, 514:514 + w], self.scr1[:, 514:514 + w],
                        self.scr2[:, 0:w], ALU.mult), waits=[(sA, at)])
                    self.op("vector", lambda eng, pr=pr, c0=c0, w=w: eng.tensor_tensor(
                        ag[:, pr * TH + c0:pr * TH + c0 + w],
                        self.scr2[:, 514:514 + w], self.Gb[:, c0:c0 + w], ALU.mult),
                        inc=(sV, 1))
                    unit += 1
            ag_done = (sV, sV.n)
            u2 = 0
            for wg in range(4):
                wb2, wrdy2 = self.wload(
                    self.d["w2t"][e * 2048:(e + 1) * 2048, wg * 512:(wg + 1) * 512]
                    .rearrange("(ic p) o -> p ic o", p=128))
                for dsub in range(4):
                    dt_ = wg * 4 + dsub
                    for gi, (c0, w) in enumerate(grps):
                        bk = 4 + u2 % 2
                        wts = list(self.guard(bk))
                        if dsub == 0 and gi == 0:
                            wts.append(wrdy2)
                        if wg == 0 and dsub == 0 and gi == 0:
                            wts.append(ag_done)
                        tl = None
                        for ic in range(NIC):
                            is_stop = (ic == NIC - 1) and e != self.nexp - 1
                            tl = self.op("tensor", lambda eng, ic=ic, dsub=dsub, c0=c0,
                                         w=w, wb2=wb2, bk=bk, st=is_stop: eng.matmul(
                                             pb[bk][:, 0:w],
                                             wb2[:, ic * 512 + dsub * 128:
                                                 ic * 512 + dsub * 128 + 128],
                                             ag[:, ic * TH + c0:ic * TH + c0 + w],
                                             start=(ic == 0), stop=st),
                                         waits=wts if ic == 0 else [],
                                         inc=(sT, 1) if (ic == NIC - 1 and e != 15) else None)
                        if e == self.nexp - 1:
                            tl = self.op("tensor", lambda eng, dt_=dt_, c0=c0, w=w,
                                         bk=bk: eng.matmul(
                                             pb[bk][:, 0:w],
                                             self.b2sb[:, dt_ * 128:(dt_ + 1) * 128],
                                             self.Gfm_b[:, c0:c0 + w],
                                             start=False, stop=True),
                                         inc=(sT, 1))
                        last_w2 = (dsub == 3 and gi == len(grps) - 1)
                        ev = self.op("vector", lambda eng, dt_=dt_, c0=c0, w=w,
                                     bk=bk: eng.tensor_tensor(
                                         X[:, dt_ * TH + c0:dt_ * TH + c0 + w],
                                         X[:, dt_ * TH + c0:dt_ * TH + c0 + w],
                                         pb[bk][:, 0:w], ALU.add),
                                     waits=[(sT, tl)], inc=(sV, 1))
                        self.setg(bk, sV, ev)
                        if last_w2:
                            self._dummy_wfree()
                        u2 += 1
        return (sV, sV.n)

    def _dummy_wfree(self, waits=()):
        w = list(waits)
        if self.sWF.n > 0:
            w.append((self.sWF, self.sWF.n))
        self.op("vector", lambda eng: eng.tensor_copy(
            self.dumc[0:1, 0:1], self.epsc[0:1, 0:1]),
            inc=(self.sWF, 1), waits=w)

    def emit_mamba(self, full, ffn_done):
        for q in range(NCHUNK):
            self._chunk(q, full, ffn_done if q == 0 else None)
        return (self.sV, self.sV.n)

    def _ln2(self, q, first_wait):
        sT, sV, sA = self.sT, self.sV, self.sA
        pb, X, R = self.pb, self.X, self.rows
        base, w = q * CH, 259
        sq_done = []
        for ic in range(NIC):
            slot = self.scr1[:, 0:w] if ic % 2 == 0 else self.scr1[:, 514:514 + w]
            wts = [] if ic >= 2 else ([first_wait] if first_wait else [])
            if ic >= 2:
                wts = [(sT, sq_done[ic - 2])]
            a_t = self.op("scalar", lambda eng, s=slot, i=ic: eng.activation(
                s, X[:, i * TH + base:i * TH + base + w], AF.Square),
                waits=wts, inc=(sA, 1))
            self.op("tensor", lambda eng, i=ic: eng.matmul(
                pb[6][0:1, 0:w], self.onesc_f[:, :],
                X[:, i * TH + base:i * TH + base + w],
                start=(i == 0), stop=(i == NIC - 1)),
                waits=(self.guard(6) + [self.setup_done, self.gload_ready]
                       + ([first_wait] if first_wait else [])) if ic == 0 else [])
            t = self.op("tensor", lambda eng, s=slot, i=ic: eng.matmul(
                pb[7][0:1, 0:w], self.onesc_f[:, :], s,
                start=(i == 0), stop=(i == NIC - 1)),
                waits=[(sA, a_t)] + (self.guard(7) if ic == 0 else []),
                inc=(sT, 1))
            sq_done.append(t)
        self.op("vector", lambda eng: eng.tensor_scalar(
            R[0:1, 0:w], pb[6][0:1, 0:w], 1.0 / 2048, None, ALU.mult),
            waits=[(sT, sq_done[-1])])
        self.op("vector", lambda eng: eng.tensor_scalar(
            R[0:1, 1024:1024 + w], pb[7][0:1, 0:w], 1.0 / 2048, None, ALU.mult))
        self.op("vector", lambda eng: eng.tensor_tensor(
            R[0:1, 512:512 + w], R[0:1, 0:w], R[0:1, 0:w], ALU.mult))
        v1 = self.op("vector", lambda eng: eng.tensor_tensor(
            R[0:1, 512:512 + w], R[0:1, 1024:1024 + w], R[0:1, 512:512 + w],
            ALU.subtract), inc=(sV, 1))
        a1 = self.op("scalar", lambda eng: eng.activation(
            R[0:1, 512:512 + w], R[0:1, 512:512 + w], AF.Ln,
            bias=self.epsc[0:1, 0:1]), waits=[(sV, v1)], inc=(sA, 1))
        a2 = self.op("scalar", lambda eng: eng.activation(
            R[0:1, 512:512 + w], R[0:1, 512:512 + w], AF.Exp, scale=-0.5),
            inc=(sA, 1))
        v3 = self.op("vector", lambda eng: eng.tensor_tensor(
            R[0:1, 1024:1024 + w], R[0:1, 0:w], R[0:1, 512:512 + w], ALU.mult),
            waits=[(sA, a2)], inc=(sV, 1))
        self.op("tensor", lambda eng: eng.matmul(
            pb[6][:, 0:w], self.ones_f[:, :], R[0:1, 512:512 + w],
            start=True, stop=True), waits=[(sV, v3)])
        t2 = self.op("tensor", lambda eng: eng.matmul(
            pb[7][:, 0:w], self.ones_f[:, :], R[0:1, 1024:1024 + w],
            start=True, stop=True), inc=(sT, 1))
        lastv = None
        for ic in range(NIC):
            self.op("vector", lambda eng, i=ic: eng.tensor_tensor(
                self.scr2[:, 0:w], X[:, i * TH + base:i * TH + base + w],
                pb[6][:, 0:w], ALU.mult), waits=[(sT, t2)] if ic == 0 else [])
            lastv = self.op("vector", lambda eng, i=ic: eng.tensor_tensor(
                self.xn2[:, i * 259:i * 259 + w], self.scr2[:, 0:w],
                pb[7][:, 0:w], ALU.subtract), inc=(sV, 1))
        self.setg(6, sV, lastv)
        self.setg(7, sV, lastv)
        if q == 0:
            lastv = self.op("vector", lambda eng: eng.tensor_scalar(
                self.xn2[:, :].rearrange("p (ic t) -> p ic t", ic=NIC)[:, :, 0:3],
                self.xn2[:, :].rearrange("p (ic t) -> p ic t", ic=NIC)[:, :, 0:3],
                self.halo_mask[:, 0:1], None, ALU.mult), inc=(sV, 1))
        return lastv, t2

    def _dt_math(self, q, bk, dt_psum_done):
        sT, sV, sA = self.sT, self.sV, self.sA
        pb = self.pb
        drain = []
        if self._mp_pairs > 0:
            drain = [(self.sFD[0], self.sFD[0].n), (self.sFD[1], self.sFD[1].n)]
        v0 = self.op("vector", lambda eng, bk=bk: eng.tensor_scalar(
            self.dt_hm[:, :], pb[bk][0:64, 0:256], self.hsmall[:, 0:1], None,
            ALU.add), waits=[(sT, dt_psum_done)] + drain, inc=(sV, 1))
        self.setg(bk, sV, v0)
        self._dummy_wfree()
        self.op("scalar", lambda eng: eng.activation(
            self.tmp_hm[:, :], self.dt_hm[:, :], AF.Abs), waits=[(sV, v0)])
        self.op("scalar", lambda eng: eng.activation(
            self.tmp_hm[:, :], self.tmp_hm[:, :], AF.Exp, scale=-1.0))
        a1 = self.op("scalar", lambda eng: eng.activation(
            self.tmp_hm[:, :], self.tmp_hm[:, :], AF.Ln,
            bias=self.onesc_f[0:64, 0:1]), inc=(sA, 1))
        self.op("vector", lambda eng: eng.tensor_scalar(
            self.dtv_hm[:, :], self.dt_hm[:, :], 0.0, None, ALU.max))
        self.op("vector", lambda eng: eng.tensor_tensor(
            self.dt_hm[:, :], self.dtv_hm[:, :], self.tmp_hm[:, :], ALU.add),
            waits=[(sA, a1)])
        self.op("vector", lambda eng: eng.tensor_scalar(
            self.tmp_hm[:, :], self.dt_hm[:, :], self.negexpA[0:64, 0:1], None,
            ALU.mult))
        v1 = self.op("vector", lambda eng: eng.tensor_tensor_scan(
            self.dAc_hm[:, :], self.tmp_hm[:, :], self.tmp_hm[:, :], 0.0,
            ALU.add, ALU.bypass), inc=(sV, 1))
        a2 = self.op("scalar", lambda eng: eng.activation(
            self.cd[:, :], self.dAc_hm[:, 255:256], AF.Exp),
            waits=[(sV, v1)], inc=(sA, 1))
        if q == 0:
            self.op("vector", lambda eng: eng.tensor_copy(
                self.cdp[:, :], self.cd[:, :]), waits=[(sA, a2)])
        else:
            self.op("vector", lambda eng: eng.tensor_tensor(
                self.cdp[:, :], self.cdp[:, :], self.cd[:, :], ALU.mult),
                waits=[(sA, a2)])
        v2 = self.op("vector", lambda eng: eng.tensor_scalar(
            self.tmp_hm[:, :], self.dAc_hm[:, :], self.dAc_hm[:, 255:256], None,
            ALU.subtract), inc=(sV, 1))
        a3 = self.op("scalar", lambda eng: eng.activation(
            self.tmp_hm[:, :], self.tmp_hm[:, :], AF.Exp, scale=-1.0),
            waits=[(sV, v2)], inc=(sA, 1))
        vdtv = self.op("vector", lambda eng: eng.tensor_tensor(
            self.dtv_hm[:, :], self.dt_hm[:, :], self.tmp_hm[:, :], ALU.mult),
            waits=[(sA, a3)], inc=(sV, 1))
        if self.which == "L2":
            ae = self.op("scalar", lambda eng: eng.activation(
                self.tmp_hm[:, :], self.dAc_hm[:, :], AF.Exp),
                waits=[(sV, vdtv)], inc=(sA, 1))
            self._ehm_ready = (sA, ae)
        for (srcT, dstT, wv) in ((self.dAc_hm, self.dAcT, v1),
                                 (self.dt_hm, self.dtT, v1),
                                 (self.dtv_hm, self.dtvT, vdtv)):
            for st in range(2):
                tt = self.op("tensor", lambda eng, s=srcT, st=st: eng.transpose(
                    pb[5][:, 0:64], s[0:64, st * 128:st * 128 + 128],
                    self.eye_f[0:64, 0:64]),
                    waits=[(sV, wv)] + self.guard(5), inc=(sT, 1))
                ve = self.op("vector", lambda eng, d=dstT, st=st: eng.tensor_copy(
                    d[:, st * 64:st * 64 + 64], pb[5][:, 0:64]),
                    waits=[(sT, tt)], inc=(sV, 1))
                self.setg(5, sV, ve)
        tt = self.op("tensor", lambda eng: eng.transpose(
            pb[5][0:1, 0:64], self.cd[:, 0:1], self.eye_f[0:64, 0:64]),
            waits=[(sA, a2)] + self.guard(5), inc=(sT, 1))
        ve = self.op("vector", lambda eng: eng.tensor_copy(
            self.rows[0:1, 1792:1856], pb[5][0:1, 0:64]),
            waits=[(sT, tt)], inc=(sV, 1))
        tt2 = self.op("tensor", lambda eng: eng.matmul(
            pb[5][:, 0:64], self.ones_f[:, :], self.rows[0:1, 1792:1856],
            start=True, stop=True), waits=[(sV, ve)], inc=(sT, 1))
        ve2 = self.op("vector", lambda eng: eng.tensor_copy(
            self.cdc[:, :], pb[5][:, 0:64]), waits=[(sT, tt2)], inc=(sV, 1))
        self.setg(5, sV, ve2)
        return ve2

    def _in_proj_conv(self, q, xn2_done):
        sT, sV, sA = self.sT, self.sV, self.sA
        pb = self.pb
        grp_list = [(4096 + 512 * k, 512) for k in range(8)] + [(8192, 320)]
        unit = 0
        xt_done = None
        dt_done = None
        for (gc0, gw) in grp_list:
            wb, wrdy = self.wload(
                self.d["iprojT"][:, gc0:gc0 + gw]
                .rearrange("(ic p) o -> p ic o", p=128))
            ntile = gw // 128 if gw % 128 == 0 else 3
            for j in range(ntile):
                ot = (gc0 - 4096) // 128 + j
                is_dt = (gc0 == 8192 and j == 2)
                tw = 64 if is_dt else 128
                bk = unit % 2
                wts = list(self.guard(bk))
                if j == 0:
                    wts.append(wrdy)
                    if xn2_done is not None and gc0 == 4096:
                        wts.append((sV, xn2_done))
                tlast = None
                for ic in range(NIC):
                    if is_dt:
                        tlast = self.op("tensor", lambda eng, ic=ic, wb=wb, gw=gw,
                                        j=j, bk=bk: eng.matmul(
                                            pb[bk][0:64, 0:256],
                                            wb[:, ic * gw + j * 128:ic * gw + j * 128 + 64],
                                            self.xn2[:, ic * 259 + 3:ic * 259 + 259],
                                            start=(ic == 0), stop=(ic == NIC - 1)),
                                        waits=wts if ic == 0 else [],
                                        inc=(sT, 1) if ic == NIC - 1 else None)
                    else:
                        tlast = self.op("tensor", lambda eng, ic=ic, wb=wb, gw=gw,
                                        j=j, bk=bk: eng.matmul(
                                            pb[bk][:, 0:259],
                                            wb[:, ic * gw + j * 128:ic * gw + j * 128 + 128],
                                            self.xn2[:, ic * 259:ic * 259 + 259],
                                            start=(ic == 0), stop=(ic == NIC - 1)),
                                        waits=wts if ic == 0 else [],
                                        inc=(sT, 1) if ic == NIC - 1 else None)
                if is_dt:
                    dt_done = self._dt_math(q, bk, tlast)
                    unit += 1
                    continue
                self.op("vector", lambda eng, bk=bk, ot=ot: eng.tensor_scalar(
                    self.convs[:, 0:256], pb[bk][:, 0:256],
                    self.convw[:, ot * 4:ot * 4 + 1], None, ALU.mult),
                    waits=[(sT, tlast)])
                self.op("vector", lambda eng, bk=bk, ot=ot: eng.scalar_tensor_tensor(
                    self.convs2[:, 0:256], pb[bk][:, 1:257],
                    self.convw[:, ot * 4 + 1:ot * 4 + 2], self.convs[:, 0:256],
                    ALU.mult, ALU.add))
                self.op("vector", lambda eng, bk=bk, ot=ot: eng.scalar_tensor_tensor(
                    self.convs[:, 0:256], pb[bk][:, 2:258],
                    self.convw[:, ot * 4 + 2:ot * 4 + 3], self.convs2[:, 0:256],
                    ALU.mult, ALU.add))
                cv = self.op("vector", lambda eng, bk=bk, ot=ot: eng.scalar_tensor_tensor(
                    self.convs2[:, 0:256], pb[bk][:, 3:259],
                    self.convw[:, ot * 4 + 3:ot * 4 + 4], self.convs[:, 0:256],
                    ALU.mult, ALU.add), inc=(sV, 1))
                self.setg(bk, sV, cv)
                if gc0 != 8192 and j == ntile - 1:
                    self._dummy_wfree()
                if ot < 32:
                    dst = self.convbf[:, 0:256]
                elif ot == 32:
                    dst = self.Bfm[:, :]
                else:
                    dst = self.Cfm[:, :]
                av = self.op("scalar", lambda eng, d=dst, ot=ot: eng.activation(
                    d, self.convs2[:, 0:256], AF.Silu,
                    bias=self.convb[:, ot:ot + 1]), waits=[(sV, cv)], inc=(sA, 1))
                self._conv_av = av
                if ot < 32 or ot == 32:
                    for st in range(2):
                        tt = self.op("tensor", lambda eng, ot=ot, st=st, d=dst:
                                     eng.transpose(
                                         self.pbf[5][:, st * 128:st * 128 + 128],
                                         d[:, st * 128:st * 128 + 128],
                                         self.eye_bf[:, :]),
                                     waits=[(sA, av)] + self.guard(5), inc=(sT, 1))
                        if ot < 32:
                            tgt = self.xT[:, st * 4096 + ot * 128:
                                          st * 4096 + ot * 128 + 128]
                        else:
                            tgt = self.Btok[:, st * 128:st * 128 + 128]
                        ve = self.op("vector", lambda eng, t=tgt, st=st: eng.tensor_copy(
                            t, self.pbf[5][:, st * 128:st * 128 + 128]),
                            waits=[(sT, tt)], inc=(sV, 1))
                        self.setg(5, sV, ve)
                        xt_done = ve
                unit += 1
        return xt_done, dt_done

    def _scan_states(self, q, xt_done, dt_done):
        sT, sV = self.sT, self.sV
        pb = self.pb
        last = None
        for hg in range(8):
            bk = hg % 2
            g = self.guard(bk)
            smm = None
            pend = None
            first_smm = True

            def emit_smm(pd):
                nonlocal smm, first_smm
                slot, jv, jw_, hp = pd
                for st in range(2):
                    smm = self.op("tensor", lambda eng, st=st, bk=bk, slot=slot,
                                  jw_=jw_: eng.matmul(
                                      pb[bk][:, slot:slot + 64],
                                      self.Btok[:, st * 128:st * 128 + 128],
                                      jw_[:, st * 64:st * 64 + 64],
                                      start=(st == 0), stop=(st == 1)),
                                  waits=[(self.sGC, jv)] +
                                  (g if (first_smm and st == 0) else []),
                                  inc=(sT, 1) if st == 1 else None)
                first_smm = False
                self._jf2[hp] = smm

            for h8 in range(8):
                h = hg * 8 + h8
                hp = h % 2
                jw_ = self.jitwb[hp]
                jfree = self._jf2[hp]
                jv = None
                for st in range(2):
                    w0 = []
                    if st == 0:
                        w0 = [(sT, jfree)]
                        if h == 0:
                            w0.append((sV, max(xt_done, dt_done)))
                    jv = self.op("gpsimd_c", lambda eng, h=h, st=st, jw_=jw_:
                                 eng.tensor_scalar(
                                     jw_[:, st * 64:st * 64 + 64],
                                     self.xT[:, st * 4096 + h * 64:st * 4096 + h * 64 + 64],
                                     self.dtvT[:, st * 64 + h:st * 64 + h + 1], None,
                                     ALU.mult),
                                 waits=w0)
                if pend is not None:
                    emit_smm(pend)
                pend = (h8 * 64, jv, jw_, hp)
            emit_smm(pend)
            for h8 in range(8):
                h = hg * 8 + h8
                slot = h8 * 64
                self.op("vector", lambda eng, h=h: eng.tensor_scalar(
                    self.prev[:, h * 64:h * 64 + 64], self.prev[:, h * 64:h * 64 + 64],
                    self.cdc[:, h:h + 1], None, ALU.mult),
                    waits=[(sT, smm)] if h8 == 0 else [])
                last = self.op("vector", lambda eng, h=h, bk=bk, slot=slot: eng.tensor_tensor(
                    self.prev[:, h * 64:h * 64 + 64], self.prev[:, h * 64:h * 64 + 64],
                    pb[bk][:, slot:slot + 64], ALU.add), inc=(sV, 1))
            self.setg(bk, sV, last)
        return last

    def _chunk(self, q, full, first_wait):
        sT, sV, sA = self.sT, self.sV, self.sA
        pb = self.pb
        base = q * CH
        if self.sGC.n > 0:
            self.wait("vector", self.sGC, self.sGC.n)
        xn2_done, ln_t2 = self._ln2(q, first_wait)
        if getattr(self, "_fix_req", False):
            self.wait("vector", sT, ln_t2)
            for ic in range(NIC):
                self.op("vector", lambda eng, ic=ic, q=q: eng.tensor_copy(
                    self.X[:, ic * TH + q * CH:ic * TH + q * CH + 3],
                    self.hsave[:, ic * 3:ic * 3 + 3]))
            self.op("vector", lambda eng: eng.tensor_copy(
                self.dumc[0:1, 0:1], self.epsc[0:1, 0:1]), inc=(self.sV, 1))
        xt_done, dt_done = self._in_proj_conv(q, xn2_done)
        if not full:
            self._scan_states(q, xt_done, dt_done)
            return
        wp_done = None
        for st in range(2):
            msk = self.mask0 if st == 0 else self.mask1
            tt = self.op("tensor", lambda eng, st=st: eng.matmul(
                pb[6][:, 0:CH], self.Bfm[:, st * 128:st * 128 + 128],
                self.Cfm[:, 0:CH], start=True, stop=True),
                waits=self.guard(6) + [(sV, xt_done), (sA, self._conv_av)],
                inc=(sT, 1))
            wp_done = self.op("vector", lambda eng, st=st, m=msk: eng.tensor_tensor(
                self.Wp[:, st * CH:(st + 1) * CH], pb[6][:, 0:CH], m[:, :],
                ALU.mult), waits=[(sT, tt)], inc=(sV, 1))
            self.setg(6, sV, wp_done)
        if getattr(self, "mstop", None) == "W":
            return
        self._prev_rdy = self.sV.n
        ssq_first = True
        last_sq = 0
        for hb in range(16):
            bD = 0 + hb % 2
            bO = 2 + hb % 2
            au_t = [0] * 4
            for hp2 in range(2):
                hh0 = hp2 * 2
                h0 = hb * 4 + hh0
                k = self._mp_pairs
                self._mp_pairs += 1
                slot = k % 2
                semD = self.sFD[slot]
                pbank = 6 if slot == 0 else 4
                offA = 1024 if slot == 0 else 0
                offB = 1536 if slot == 0 else 512
                war = self._mp_war[slot]
                dw = [(sV, wp_done), self._ehm_ready]
                if war is not None:
                    dw.append((sT, war))
                for j in range(2):
                    self.op("gpsimd", lambda eng, h=h0, j=j, offA=offA:
                            eng.dma_start(
                                out=self.rows[0:1, offA + j * CH:
                                              offA + (j + 1) * CH],
                                in_=self.dAc_hm[h + j:h + j + 1, 0:CH]),
                            inc=(semD, 16), waits=dw if j == 0 else [])
                for j in range(2):
                    self.op("gpsimd", lambda eng, h=h0, j=j, offB=offB:
                            eng.dma_start(
                                out=self.rows[0:1, offB + j * CH:
                                              offB + (j + 1) * CH],
                                in_=self.tmp_hm[h + j:h + j + 1, 0:CH]),
                            inc=(semD, 16))
                dtgt = semD.n
                bcp = self.op("tensor", lambda eng, offA=offA, pbank=pbank:
                              eng.matmul(
                                  pb[pbank][:, 0:512], self.ones_f[:, :],
                                  self.rows[0:1, offA:offA + 512],
                                  start=True, stop=True),
                              waits=self.guard(pbank) + [(semD, dtgt)],
                              inc=(sT, 1))
                pinfo = []
                dv = None
                for j in range(2):
                    hh = hh0 + j
                    h = hb * 4 + hh
                    hp = h % 2
                    Mp_ = self.Mpb[hp]
                    jfree = self._jf2[hp]
                    poff = j * CH
                    mpt = None
                    for st in range(2):
                        dw2 = [(sT, bcp)] if (st == 0 and j == 0) else []
                        dv = self.op("vector", lambda eng, h=h, st=st, poff=poff,
                                     pbank=pbank: eng.tensor_scalar(
                                         self.scr1[:, 0:CH],
                                         pb[pbank][:, poff:poff + CH],
                                         self.dAcT[:, st * 64 + h:st * 64 + h + 1],
                                         0.0, ALU.subtract, ALU.min),
                                     waits=dw2, inc=(sV, 1))
                        ax = self.op("scalar", lambda eng: eng.activation(
                            self.scr1[:, 514:514 + CH], self.scr1[:, 0:CH],
                            AF.Exp), waits=[(sV, dv)], inc=(sA, 1))
                        mpt = self.op("vector", lambda eng, st=st, Mp_=Mp_:
                                      eng.tensor_tensor(
                                          Mp_[:, st * CH:(st + 1) * CH],
                                          self.scr1[:, 514:514 + CH],
                                          self.Wp[:, st * CH:(st + 1) * CH],
                                          ALU.mult),
                                      waits=[(sA, ax), (sV, wp_done)] +
                                      ([(sT, jfree)] if st == 0 else []))
                    pinfo.append((hh, h, hp, jfree, mpt))
                for (hh, h, hp, jfree, mpt) in pinfo:
                    Mp_, jw_, jD_ = self.Mpb[hp], self.jitwb[hp], self.jitDb[hp]
                    jw0 = [(sT, jfree)]
                    if hb == 0 and hh == 0:
                        jw0.append((sV, max(xt_done, dt_done)))
                        if getattr(self, "_gc_first", True):
                            jw0 += [self.gload_ready, self.consts]
                            self._gc_first = False
                    for st in range(2):
                        self.op("gpsimd_c", lambda eng, h=h, st=st, jw_=jw_:
                                eng.tensor_scalar(
                                    jw_[:, st * 64:st * 64 + 64],
                                    self.xT[:, st * 4096 + h * 64:
                                            st * 4096 + h * 64 + 64],
                                    self.dtT[:, st * 64 + h:st * 64 + h + 1],
                                    None, ALU.mult),
                                waits=jw0 if st == 0 else [])
                    self.op("gpsimd_c", lambda eng, h=h, jD_=jD_:
                            eng.tensor_scalar(
                                jD_[:, :], self.eye_bf[:, :],
                                self.dcol128[:, h:h + 1], None, ALU.mult))
                    jv = self.op("gpsimd_c", lambda eng, h=h, jw_=jw_:
                                 eng.tensor_copy(
                                     jw_[:, 128:192],
                                     self.prev[:, h * 64:h * 64 + 64]),
                                 waits=[(sV, self._prev_rdy)]
                                 if (hh == 0 and hb == 0) else [])
                    rowsl = (hh % 2) * 64
                    colsl = (hh // 2) * 256
                    gD = self.guard(bD) if hh == 0 else []
                    for st in range(2):
                        self.op("tensor", lambda eng, st=st, bD=bD, rowsl=rowsl,
                                colsl=colsl, Mp_=Mp_, jw_=jw_: eng.matmul(
                                    pb[bD][rowsl:rowsl + 64, colsl:colsl + CH],
                                    jw_[:, st * 64:st * 64 + 64],
                                    Mp_[:, st * CH:(st + 1) * CH],
                                    start=(st == 0), stop=False),
                                waits=([(self.sGC, jv), (sV, mpt)] + gD)
                                if st == 0 else [])
                    for st in range(2):
                        self.op("tensor", lambda eng, st=st, bD=bD, rowsl=rowsl,
                                colsl=colsl, h=h, jD_=jD_: eng.matmul(
                                    pb[bD][rowsl:rowsl + 64,
                                           colsl + st * 128:colsl + st * 128 + 128],
                                    self.xT[:, st * 4096 + h * 64:
                                            st * 4096 + h * 64 + 64],
                                    jD_[:, :],
                                    start=False, stop=(st == 1)))
                    gO = self.guard(bO) if hh == 0 else []
                    yoff_t = self.op("tensor",
                            lambda eng, bO=bO, rowsl=rowsl, colsl=colsl, jw_=jw_:
                            eng.matmul(
                                pb[bO][rowsl:rowsl + 64, colsl:colsl + CH],
                                jw_[:, 128:192], self.Cfm[:, 0:CH],
                                start=True, stop=True),
                            waits=gO, inc=(sT, 1))
                    self._jf2[hp] = yoff_t
                bc2p = self.op("tensor", lambda eng, offB=offB, pbank=pbank:
                               eng.matmul(
                                   pb[pbank][:, 0:512], self.ones_f[:, :],
                                   self.rows[0:1, offB:offB + 512],
                                   start=True, stop=True),
                               waits=[(sV, dv)], inc=(sT, 1))
                for j2 in range(2):
                    hh2 = hh0 + j2
                    ub2 = self.convs2 if hh2 // 2 == 0 else self.scr2
                    au_t[hh2] = self.op(
                        "vector", lambda eng, u=ub2, hh2=hh2, j2=j2,
                        pbank=pbank: eng.tensor_copy(
                            u[(hh2 % 2) * 64:(hh2 % 2) * 64 + 64, 0:CH],
                            pb[pbank][(hh2 % 2) * 64:(hh2 % 2) * 64 + 64,
                                      j2 * CH:j2 * CH + CH]),
                        waits=[(sT, bc2p)] if j2 == 0 else [], inc=(sV, 1))
                self.setg(pbank, sV, au_t[hh0 + 1])
                self._mp_war[slot] = bc2p
            ymm_done = self.sT.n
            if getattr(self, "mstop", None) in ("M1", "Y0"):
                fv = self.op("vector", lambda eng, bD=bD, bO=bO: eng.tensor_copy(
                    self.dumc[0:1, 0:1], self.epsc[0:1, 0:1]),
                    waits=[(sT, ymm_done)], inc=(sV, 1))
                self.setg(bD, sV, fv)
                self.setg(bO, sV, fv)
                continue
            for pi in range(2):
                p2 = hb * 2 + pi
                if p2 % 4 == 0:
                    self.zwb, self.zrdy = self.wload(
                        self.d["iprojT"][:, (p2 // 4) * 512:(p2 // 4) * 512 + 512]
                        .rearrange("(ic p) o -> p ic o", p=128))
                bz = 4 + p2 % 2
                gz = self.guard(bz)
                zl = None
                for ic in range(NIC):
                    zl = self.op("tensor", lambda eng, ic=ic, p2=p2, bz=bz,
                                 zwb=self.zwb: eng.matmul(
                                     pb[bz][:, 0:CH],
                                     zwb[:, ic * 512 + (p2 % 4) * 128:
                                         ic * 512 + (p2 % 4) * 128 + 128],
                                     self.xn2[:, ic * 259 + 3:ic * 259 + 259],
                                     start=(ic == 0), stop=(ic == NIC - 1)),
                                 waits=(gz + [self.zrdy]) if ic == 0 else [],
                                 inc=(sT, 1) if ic == NIC - 1 else None)
                if p2 % 4 == 3:
                    self._dummy_wfree(waits=[(sT, zl)])
                zev = self.op("vector", lambda eng, bz=bz: eng.tensor_copy(
                    self.convs[:, 0:CH], pb[bz][:, 0:CH]),
                    waits=[(sT, zl)], inc=(sV, 1))
                self.setg(bz, sV, zev)
                az = self.op("scalar", lambda eng: eng.activation(
                    self.convs[:, 0:CH], self.convs[:, 0:CH], AF.Silu),
                    waits=[(sV, zev)], inc=(sA, 1))
                if self._sq_pend is not None:
                    psq, pslot, pfirst = self._sq_pend
                    self._ssqt[pslot] = self.op(
                        "tensor", lambda eng, pslot=pslot, pfirst=pfirst:
                        eng.matmul(pb[7][0:1, 0:CH], self.onesc_f[:, :],
                                   self.scr2[:, 514 + pslot * CH:
                                             514 + pslot * CH + CH],
                                   start=pfirst, stop=False),
                        waits=[(sA, psq)] + (self.guard(7) if pfirst else []),
                        inc=(sT, 1))
                    self._sq_pend = None
                colsl = pi * 256
                aw_v = max(au_t[2 * pi], au_t[2 * pi + 1])
                for hh2 in range(2):
                    h = p2 * 2 + hh2
                    rowsl = hh2 * 64
                    ub = self.convs2 if pi == 0 else self.scr2
                    self.op("vector", lambda eng, bO=bO, rowsl=rowsl, colsl=colsl,
                            u=ub: eng.tensor_tensor(
                                self.ysc[rowsl:rowsl + 64, 0:CH],
                                pb[bO][rowsl:rowsl + 64, colsl:colsl + CH],
                                u[rowsl:rowsl + 64, 0:CH], ALU.mult),
                            waits=[(sT, ymm_done), (sV, aw_v), (sA, last_sq)]
                            if hh2 == 0 else [])
                    self.op("vector", lambda eng, bD=bD, rowsl=rowsl, colsl=colsl:
                            eng.tensor_tensor(
                                self.ysc[rowsl:rowsl + 64, 0:CH],
                                self.ysc[rowsl:rowsl + 64, 0:CH],
                                pb[bD][rowsl:rowsl + 64, colsl:colsl + CH], ALU.add))
                ygv = self.op("vector", lambda eng: eng.tensor_tensor(
                    self.ysc[:, 0:CH], self.ysc[:, 0:CH], self.convs[:, 0:CH],
                    ALU.mult), waits=[(sA, az)], inc=(sV, 1))
                if pi == 1:
                    self.setg(bD, sV, ygv)
                    self.setg(bO, sV, ygv)
                slot2 = p2 % 2
                sq = self.op("scalar", lambda eng, slot2=slot2: eng.activation(
                    self.scr2[:, 514 + slot2 * CH:514 + slot2 * CH + CH],
                    self.ysc[:, 0:CH], AF.Square),
                    waits=[(sV, ygv), (sT, self._ssqt[slot2])], inc=(sA, 1))
                self._sq_pend = (sq, slot2, ssq_first)
                ssq_first = False
                last_sq = sq
                self.op("vector", lambda eng, p2=p2: eng.tensor_scalar(
                    self.ynfm[:, p2 * CH:(p2 + 1) * CH], self.ysc[:, 0:CH],
                    self.nwattn[:, p2:p2 + 1], None, ALU.mult))
        psq, pslot, pfirst = self._sq_pend
        self._ssqt[pslot] = self.op(
            "tensor", lambda eng, pslot=pslot, pfirst=pfirst:
            eng.matmul(pb[7][0:1, 0:CH], self.onesc_f[:, :],
                       self.scr2[:, 514 + pslot * CH:514 + pslot * CH + CH],
                       start=pfirst, stop=True),
            waits=[(sA, psq)] + (self.guard(7) if pfirst else []),
            inc=(sT, 1))
        self._sq_pend = None
        ssq_mm = self.sT.n
        if getattr(self, "mstop", None) in ("Y", "M1", "Y0"):
            return
        self._scan_states(q, xt_done, dt_done)
        if getattr(self, "mstop", None) == "S":
            return
        R = self.rows
        v1 = self.op("vector", lambda eng: eng.tensor_scalar(
            R[0:1, 1536:1536 + CH], pb[7][0:1, 0:CH], 1.0 / 4096, EPS,
            ALU.mult, ALU.add), waits=[(sT, ssq_mm)], inc=(sV, 1))
        self.setg(7, sV, v1)
        self.op("scalar", lambda eng: eng.activation(
            R[0:1, 1536:1536 + CH], R[0:1, 1536:1536 + CH], AF.Ln),
            waits=[(sV, v1)])
        a1 = self.op("scalar", lambda eng: eng.activation(
            R[0:1, 1536:1536 + CH], R[0:1, 1536:1536 + CH], AF.Exp, scale=-0.5),
            inc=(sA, 1))
        bc = self.op("tensor", lambda eng: eng.matmul(
            pb[6][:, 0:CH], self.ones_f[:, :], R[0:1, 1536:1536 + CH],
            start=True, stop=True), waits=[(sA, a1)] + self.guard(6), inc=(sT, 1))
        rs = self.op("vector", lambda eng: eng.tensor_copy(
            self.ysc[:, 0:CH], pb[6][:, 0:CH]), waits=[(sT, bc)], inc=(sV, 1))
        self.setg(6, sV, rs)
        for dg in range(8):
            wb3, wrdy3 = self.wload(
                self.d["oprojT"][:, dg * 256:(dg + 1) * 256]
                .rearrange("(icc p) o -> p icc o", p=128))
            for dsub in range(2):
                dt_ = dg * 2 + dsub
                bk = 2 + (dt_ % 2)
                wts = list(self.guard(bk))
                if dsub == 0:
                    wts.append(wrdy3)
                if dg == 0 and dsub == 0:
                    wts.append((sV, rs))
                tl = None
                for icc in range(32):
                    tl = self.op("tensor", lambda eng, icc=icc, dsub=dsub, bk=bk,
                                 wb3=wb3: eng.matmul(
                                     pb[bk][:, 0:CH],
                                     wb3[:, icc * 256 + dsub * 128:
                                         icc * 256 + dsub * 128 + 128],
                                     self.ynfm[:, icc * CH:(icc + 1) * CH],
                                     start=(icc == 0), stop=(icc == 31)),
                                 waits=wts if icc == 0 else [],
                                 inc=(sT, 1) if icc == 31 else None)
                self.op("vector", lambda eng, bk=bk: eng.tensor_tensor(
                    self.scr2[:, 0:CH], pb[bk][:, 0:CH], self.ysc[:, 0:CH],
                    ALU.mult), waits=[(sT, tl), (sV, rs)])
                if q < 3:
                    ev = self.op("vector", lambda eng, dt_=dt_, base=base:
                                 eng.tensor_tensor(
                                     self.X[:, dt_ * TH + base + 3:dt_ * TH + base + 256],
                                     self.X[:, dt_ * TH + base + 3:dt_ * TH + base + 256],
                                     self.scr2[:, 0:253], ALU.add))
                    ev = self.op("vector", lambda eng, dt_=dt_, base=base:
                                 eng.tensor_tensor(
                                     self.hsave[:, dt_ * 3:dt_ * 3 + 3],
                                     self.X[:, dt_ * TH + base + 256:dt_ * TH + base + 259],
                                     self.scr2[:, 253:256], ALU.add),
                                 inc=(sV, 1))
                else:
                    ev = self.op("vector", lambda eng, dt_=dt_, base=base:
                                 eng.tensor_tensor(
                                     self.X[:, dt_ * TH + base + 3:dt_ * TH + base + 259],
                                     self.X[:, dt_ * TH + base + 3:dt_ * TH + base + 259],
                                     self.scr2[:, 0:256], ALU.add),
                                 inc=(sV, 1))
                self.setg(bk, sV, ev)
                if dsub == 1:
                    self._dummy_wfree()
        self.resid_done = (sV, self.sV.n)

    def _prev_init(self):
        sT, sV, sA = self.sT, self.sV, self.sA
        pb = self.pb
        d = self.d
        self.op("vector", lambda eng: eng.memset(self.prev[:, :], 0.0),
                waits=[self.consts], inc=(sV, 1))
        last_upd = None
        for j in range(7):
            ld = self.op("sync", lambda eng, j=j: eng.dma_start(
                out=self.pRscr[:, :], in_=d["prevR"][j * 128:(j + 1) * 128, :]),
                inc=(self.sIO, 16),
                waits=([(sV, last_upd)] if last_upd else [self.x_ready]))
            cd = self.op("sync", lambda eng, j=j: eng.dma_start(
                out=self.cd[:, 0:1], in_=d["prevcd"][j * 64:(j + 1) * 64, :]),
                inc=(self.sIO, 16))
            ldv = self.sIO.n
            self.op("vector", lambda eng, j=j: eng.tensor_tensor(
                self.cd[:, 0:1], self.cd[:, 0:1], self.pmaskS[0:64, j:j + 1],
                ALU.mult), waits=[(self.sIO, ldv)])
            v0 = self.op("vector", lambda eng, j=j: eng.tensor_scalar(
                self.smallT[:, 0:1], self.pmaskS[0:64, j:j + 1], -1.0, 1.0,
                ALU.mult, ALU.add), inc=(sV, 1))
            v1 = self.op("vector", lambda eng: eng.tensor_tensor(
                self.cd[:, 0:1], self.cd[:, 0:1], self.smallT[:, 0:1], ALU.add),
                inc=(sV, 1))
            tt = self.op("tensor", lambda eng: eng.transpose(
                pb[5][0:1, 0:64], self.cd[:, 0:1], self.eye_f[0:64, 0:64]),
                waits=[(sV, v1)] + self.guard(5), inc=(sT, 1))
            ve = self.op("vector", lambda eng: eng.tensor_copy(
                self.rows[0:1, 1792:1856], pb[5][0:1, 0:64]),
                waits=[(sT, tt)], inc=(sV, 1))
            tt2 = self.op("tensor", lambda eng: eng.matmul(
                pb[5][:, 0:64], self.ones_f[:, :], self.rows[0:1, 1792:1856],
                start=True, stop=True), waits=[(sV, ve)], inc=(sT, 1))
            ve2 = self.op("vector", lambda eng: eng.tensor_copy(
                self.cdc[:, :], pb[5][:, 0:64]), waits=[(sT, tt2)], inc=(sV, 1))
            self.setg(5, sV, ve2)
            self.op("vector", lambda eng, j=j: eng.tensor_scalar(
                self.pRscr[:, :], self.pRscr[:, :], self.pmaskS[:, j:j + 1],
                None, ALU.mult))
            for h in range(NH):
                self.op("vector", lambda eng, h=h: eng.tensor_scalar(
                    self.prev[:, h * 64:h * 64 + 64],
                    self.prev[:, h * 64:h * 64 + 64],
                    self.cdc[:, h:h + 1], None, ALU.mult))
                self.op("vector", lambda eng, h=h: eng.tensor_tensor(
                    self.prev[:, h * 64:h * 64 + 64],
                    self.prev[:, h * 64:h * 64 + 64],
                    self.pRscr[:, h * 64:h * 64 + 64], ALU.add))
            last_upd = self.op("vector", lambda eng: eng.tensor_copy(
                self.dumc[0:1, 0:1], self.epsc[0:1, 0:1]), inc=(sV, 1))

    def emit_mamba(self, full, first_wait):
        nch = getattr(self, "nchunks", NCHUNK)
        if not full:
            self.op("vector", lambda eng: eng.memset(self.prev[:, :], 0.0),
                    waits=[first_wait], inc=(self.sV, 1))
        for q in range(nch):
            self._chunk2(q, full, first_wait if q == 0 else None,
                         fix_prev=(full and q > 0))
        return (self.sV, self.sV.n)

    def _chunk2(self, q, full, first_wait, fix_prev):
        self._fix_req = fix_prev
        self._chunk(q, full, first_wait)

    def finalize(self):
        with self.nc.Block() as block:
            @block.tensor
            def _(eng):
                for fn in self.q["tensor"]:
                    fn(eng)

            @block.vector
            def _(eng):
                for fn in self.q["vector"]:
                    fn(eng)

            @block.scalar
            def _(eng):
                for fn in self.q["scalar"]:
                    fn(eng)

            @block.gpsimd
            def _(eng):
                for fn in self.q["gpsimd"]:
                    fn(eng)

            @block.sync
            def _(eng):
                for fn in self.q["sync"]:
                    fn(eng)
        return self.nc


def build_l1(debug_stop=None, grps=None, nexp=NEXP, stages="all"):
    p = P("L1", debug_stop, nexp=nexp)
    p.emit_setup()
    ffn_done = p.emit_ffn(grps or FFN1_GRPS, p.x_ready, stages=stages)
    ffn_T = (p.sT, p.sT.n)
    p.op("sync", lambda eng: eng.dma_start(out=p.d["x1out"], in_=p.X[:, :]),
         waits=[ffn_done], inc=(p.sST, 16))
    if debug_stop == "ffn1":
        p.op("sync", lambda eng: eng.dma_start(out=p.d["xnD"], in_=p.xn[:, :]),
             inc=(p.sST, 16))
        p.op("sync", lambda eng: eng.dma_start(out=p.d["gpackD"][0:16, :], in_=p.logit),
             inc=(p.sST, 16))
        p.op("sync", lambda eng: eng.dma_start(out=p.d["gpackD"][16:32, :], in_=p.Gfm_a),
             inc=(p.sST, 16))
        p.op("sync", lambda eng: eng.dma_start(out=p.d["gpackD"][32:48, :], in_=p.Gfm_b),
             inc=(p.sST, 16))
        p.op("sync", lambda eng: eng.dma_start(out=p.d["GbD"], in_=p.Gb[:, :]),
             inc=(p.sST, 16))
        p.op("sync", lambda eng: eng.dma_start(out=p.d["agD"], in_=p.ag[:, :]),
             inc=(p.sST, 16))
    if debug_stop != "ffn1":
        mdone = p.emit_mamba(False, ffn_T)
        p.op("sync", lambda eng: eng.dma_start(out=p.d["Rout"], in_=p.prev[:, :]),
             waits=[mdone], inc=(p.sST, 16))
        p.op("sync", lambda eng: eng.dma_start(out=p.d["cdout"], in_=p.cdp[:, :]),
             waits=[(p.sV, p.sV.n)], inc=(p.sST, 16))
    p.wait("sync", p.sST, p.sST.n)
    return p.finalize()


def build_l2(nexp=NEXP, mstop=None, ffn2=True, states_only=False, nchunks=None, msub=None):
    p = P("L2", nexp=nexp)
    p.mstop = mstop
    p.msub = msub
    if nchunks:
        p.nchunks = nchunks
    p.emit_setup()
    p._prev_init()
    mdone = p.emit_mamba(not states_only, p.x_ready)
    if ffn2 and not states_only and mstop is None:
        ffn_done = p.emit_ffn(FFN2_GRPS, mdone)
    else:
        ffn_done = mdone
    xo = p.d["xout"].rearrange("p (ic t) -> p ic t", ic=NIC)
    xi = p.X[:, :].rearrange("p (ic t) -> p ic t", ic=NIC)[:, :, 3:1027]
    p.op("sync", lambda eng: eng.dma_start(out=xo, in_=xi),
         waits=[ffn_done], inc=(p.sST, 16))
    p.wait("sync", p.sST, p.sST.n)
    return p.finalize()



def host_perm_ab():
    ev = np.arange(0, 4096, 2)
    od = np.arange(1, 4096, 2)
    blocks = []
    for k in range(16):
        blocks.append(ev[k * 128:(k + 1) * 128])
        blocks.append(od[k * 128:(k + 1) * 128])
    return np.concatenate(blocks)


def host_weights(inp):
    d = {}
    perm = host_perm_ab()
    w1 = np.asarray(inp["L0_ffn_w1"])
    d["w1tp"] = np.ascontiguousarray(
        w1.transpose(0, 2, 1)[:, :, perm].reshape(16 * 2048, 4096))
    w2 = np.asarray(inp["L0_ffn_w2"])
    d["w2t"] = np.ascontiguousarray(w2.transpose(0, 2, 1).reshape(16 * 2048, 2048))
    d["iprojT"] = np.ascontiguousarray(np.asarray(inp["L0_attn_in_proj_w"]).T)
    d["oprojT"] = np.ascontiguousarray(np.asarray(inp["L0_attn_out_proj_w"]).T)
    gw = np.asarray(inp["L0_ffn_gate_w"])
    d["gwt"] = np.ascontiguousarray(
        gw.T.reshape(NIC, 128, 16).transpose(1, 0, 2).reshape(128, NIC * 16))
    d["gateb"] = np.ascontiguousarray(np.asarray(inp["L0_ffn_gate_b"]).reshape(1, 16).astype(np.float32))
    b1 = np.asarray(inp["L0_ffn_b1"])[:, perm]
    d["b1sb"] = np.ascontiguousarray(
        b1.reshape(16, 32, 128).transpose(2, 0, 1).reshape(128, 16 * 32))
    d["b2sb"] = np.ascontiguousarray(np.asarray(inp["L0_ffn_b2"]))
    cw = np.asarray(inp["L0_attn_conv_w"])
    d["convw"] = np.ascontiguousarray(
        cw.reshape(34, 128, 4).transpose(1, 0, 2).reshape(128, 34 * 4))
    d["convb"] = np.ascontiguousarray(
        np.asarray(inp["L0_attn_conv_b"]).reshape(34, 128).T)
    d["hsmall"] = np.ascontiguousarray(np.stack(
        [np.asarray(inp["L0_attn_dt_bias"]), np.asarray(inp["L0_attn_A_log"]),
         np.asarray(inp["L0_attn_D"])], axis=1).astype(np.float32))
    d["nwffn"] = np.ascontiguousarray(
        np.asarray(inp["L0_ffn_norm_w"]).reshape(NIC, 128).T)
    anw = np.asarray(inp["L0_attn_norm_weight"])
    d["nwattn"] = np.ascontiguousarray(anw.reshape(32, 128).T)
    D = np.asarray(inp["L0_attn_D"]).astype(np.float32)
    d["dcol128"] = np.ascontiguousarray(np.broadcast_to(D[None, :], (128, 64)))
    m0 = np.zeros((128, CH), np.float32)
    m1 = np.zeros((128, CH), np.float32)
    for s in range(128):
        m0[s, s:] = 1.0
        if 128 + s < CH:
            m1[s, 128 + s:] = 1.0
        m1[s, 128 + s:] = 1.0
    d["mask0"], d["mask1"] = m0, m1
    d["eye_f"] = np.eye(128, dtype=np.float32)
    d["eye_bf"] = np.eye(128, dtype=np.float32).astype(ml_dtypes.bfloat16)
    return d


def host_core_inputs(inp, core):
    u = np.asarray(inp["L0_attn_u"])[0]
    uT = u.T
    t0 = core * TMAIN
    cols = np.zeros((2048, TH), np.float32)
    if core > 0:
        cols[:, 0:3] = uT[:, t0 - 3:t0]
    cols[:, 3:3 + TMAIN] = uT[:, t0:t0 + TMAIN]
    d = {}
    d["uT"] = np.ascontiguousarray(
        cols.reshape(NIC, 128, TH).transpose(1, 0, 2).reshape(128, NIC * TH))
    d["halo_mask"] = np.full((128, 1), 0.0 if core == 0 else 1.0, np.float32)
    return d


def kernel(**inputs):
    wts = host_weights(inputs)
    nc1 = build_l1()
    in_maps1 = []
    for c in range(NCORE):
        m = dict(wts)
        m.pop("oprojT"); m.pop("nwattn"); m.pop("dcol128")
        m.update(host_core_inputs(inputs, c))
        in_maps1.append(m)
    r1 = run_bass_kernel_spmd(nc1, in_maps1, core_ids=list(range(NCORE)))
    nc2 = build_l2()
    in_maps2 = []
    for c in range(NCORE):
        m = dict(wts)
        m["x1in"] = r1.results[c]["x1out"]
        pR = np.zeros((7 * 128, DIN), np.float32)
        pcd = np.zeros((7 * 64, 1), np.float32)
        pm = np.zeros((128, 7), np.float32)
        for j in range(7):
            if j < c:
                pR[j * 128:(j + 1) * 128] = r1.results[j]["Rout"]
                pcd[j * 64:(j + 1) * 64] = r1.results[j]["cdout"]
                pm[:, j] = 1.0
        m["prevR"], m["prevcd"], m["pmask"] = pR, pcd, pm
        m["halo_mask"] = in_maps1[c]["halo_mask"]
        in_maps2.append(m)
    r2 = run_bass_kernel_spmd(nc2, in_maps2, core_ids=list(range(NCORE)))
    outs = []
    for c in range(NCORE):
        o = r2.results[c]["xout"]
        outs.append(o.reshape(128, NIC, TMAIN).transpose(1, 0, 2).reshape(2048, TMAIN))
    full = np.concatenate(outs, axis=1)
    return np.ascontiguousarray(full.T)[None]


# version 09
# speedup vs baseline: 1.0201x; 1.0201x over previous
import numpy as np
import ml_dtypes
import concourse.bass as bass
import concourse.mybir as mybir
from concourse.bass_utils import run_bass_kernel_spmd

F32 = mybir.dt.float32
BF16 = mybir.dt.bfloat16
AF = mybir.ActivationFunctionType
ALU = mybir.AluOpType
AX = mybir.AxisListType

NCORE = 8
TMAIN = 1024
TH = 1028
NIC = 16
NEXP = 16
CH = 256
NCHUNK = 4
NH = 64
HP = 64
DIN = 4096
EPS = 1e-5
SBUF_BASE = 16512
SBUF_TOP = 229344

FFN1_GRPS = [(0, 257), (257, 257), (514, 257), (771, 257)]
FFN2_GRPS = [(3, 512), (515, 512)]


class Sem:
    def __init__(self, nc, name):
        self.h = nc.alloc_semaphore(name)
        self.n = 0


class P:
    def __init__(self, which, debug_stop=None, nexp=NEXP):
        self.which = which
        self.nexp = nexp
        self.debug_stop = debug_stop
        nc = bass.Bass("TRN2", target_bir_lowering=False)
        self.nc = nc
        self.q = {e: [] for e in ("tensor", "vector", "scalar", "gpsimd", "sync")}
        for s in ("sT", "sV", "sA", "sG", "sIO", "sWL0", "sWL1", "sWF", "sST"):
            setattr(self, s, Sem(nc, s))
        self.sFD = [Sem(nc, "sFD0"), Sem(nc, "sFD1")]
        self.sGC = Sem(nc, "sGC")
        self._sq_pend = None
        self._ssqt = [0, 0]
        self._mp_pairs = 0
        self._mp_war = [None, None]
        self.sb_off = SBUF_BASE
        self.wload_n = 0
        self._jit_free = 0
        self._jf2 = [0, 0]
        self.bankfree = [None] * 8

        self._declare_dram()
        self._declare_sbuf()
        self.pb = [nc.alloc_psum_tensor(f"pbank{i}", [128, 512], F32) for i in range(8)]
        self.pbf = [p.bitcast(BF16) for p in self.pb]

    def _declare_dram(self):
        nc = self.nc
        DI = lambda n, sh, dt=F32: nc.dram_tensor(n, sh, dt, kind="ExternalInput").ap()
        DO = lambda n, sh, dt=F32: nc.dram_tensor(n, sh, dt, kind="ExternalOutput").ap()
        d = {}
        if self.which == "L1":
            d["uT"] = DI("uT", [128, NIC * TH])
            d["x1out"] = DO("x1out", [128, NIC * TH])
            if self.debug_stop != "ffn1":
                d["Rout"] = DO("Rout", [128, DIN])
                d["cdout"] = DO("cdout", [64, 1])
            else:
                d["xnD"] = DO("xnD", [128, NIC * TH], BF16)
                d["gpackD"] = DO("gpackD", [48, TH])
                d["GbD"] = DO("GbD", [128, TH])
                d["agD"] = DO("agD", [128, NIC * TH], BF16)
        else:
            d["x1in"] = DI("x1in", [128, NIC * TH])
            d["prevR"] = DI("prevR", [7 * 128, DIN])
            d["prevcd"] = DI("prevcd", [7 * 64, 1])
            d["pmask"] = DI("pmask", [128, 7])
            d["oprojT"] = DI("oprojT", [4096, 2048])
            d["nwattn"] = DI("nwattn", [128, 32])
            d["dcol128"] = DI("dcol128", [128, 64])
            d["xout"] = DO("xout", [128, NIC * TMAIN])
        d["w1tp"] = DI("w1tp", [NEXP * 2048, 4096])
        d["w2t"] = DI("w2t", [NEXP * 2048, 2048])
        d["iprojT"] = DI("iprojT", [2048, 8512])
        d["gwt"] = DI("gwt", [128, NIC * 16])
        d["gateb"] = DI("gateb", [1, 16])
        d["b1sb"] = DI("b1sb", [128, 16 * 32])
        d["b2sb"] = DI("b2sb", [16, 2048])
        d["convw"] = DI("convw", [128, 34 * 4])
        d["convb"] = DI("convb", [128, 34])
        d["hsmall"] = DI("hsmall", [64, 3])
        d["nwffn"] = DI("nwffn", [128, NIC])
        d["mask0"] = DI("mask0", [128, CH])
        d["mask1"] = DI("mask1", [128, CH])
        d["eye_f"] = DI("eye_f", [128, 128])
        d["eye_bf"] = DI("eye_bf", [128, 128], BF16)
        d["halo_mask"] = DI("halo_mask", [128, 1])
        self.d = d

    def sb(self, name, shape, dtype, align=64):
        esz = 2 if dtype == BF16 else 4
        self.sb_off = (self.sb_off + align - 1) // align * align
        t = self.nc.alloc_sbuf_tensor_at(name, shape, dtype, offset=self.sb_off)
        self.sb_off += shape[1] * esz
        assert self.sb_off <= SBUF_TOP, f"sbuf overflow at {name}: {self.sb_off}"
        return t

    def _declare_sbuf(self):
        sb = self.sb
        self.X = sb("X", [128, NIC * TH], F32)
        self.wbuf = [sb(f"wbuf{i}", [128, 8192], BF16) for i in range(2)]
        self.scr1 = sb("scr1", [128, TH], F32)
        self.scr2 = sb("scr2", [128, TH], F32)
        self.Gb = sb("GbS", [128, TH], F32)
        self.Gtt = self.nc.alloc_sbuf_tensor_at(
            "GttS", [128, 9 * 16], F32,
            offset=self.sb_off - TH * 4)
        gbo = self.sb_off - TH * 4
        self.Mp2 = self.nc.alloc_sbuf_tensor_at(
            "Mp2S", [128, 2 * CH], BF16, offset=gbo + 1024)
        self.jitw2 = self.nc.alloc_sbuf_tensor_at(
            "jitw2S", [128, 192], BF16, offset=gbo + 2048)
        self.jitD2 = self.nc.alloc_sbuf_tensor_at(
            "jitD2S", [128, 128], BF16, offset=gbo + 2432)
        self.rows = sb("rows", [1, 2048], F32)
        self.onesrow = sb("onesrow", [1, 512], F32)
        self.gpack = sb("gpack", [128, TH + 2048], F32)
        self.logit = self.gpack[0:16, 0:TH]
        self.Gfm_a = self.gpack[32:48, 0:TH]
        self.Gfm_b = self.gpack[64:80, 0:TH]
        self.b2sb = self.gpack[64:80, TH:TH + 2048]
        self.ones_f = sb("ones_f", [1, 128], F32)
        self.onesc_f = sb("onesc_f", [128, 1], F32)
        self.epsc = sb("epsc", [1, 1], F32)
        self.dumc = sb("dumc", [1, 1], F32)
        self.eye_f = sb("eye_fS", [128, 128], F32)
        self.eye_bf = sb("eye_bfS", [128, 128], BF16)
        self.b1sb = sb("b1sbS", [128, 512], F32)
        self.gwt = sb("gwtS", [128, NIC * 16], BF16)
        self.gateb = sb("gatebS", [1, 16], F32)
        self.convw = sb("convwS", [128, 136], F32)
        self.convb = sb("convbS", [128, 34], F32)
        self.hsmall = sb("hsmallS", [64, 3], F32)
        self.negexpA = sb("negexpA", [64, 1], F32)
        self.nwffn = sb("nwffnS", [128, NIC], F32)
        self.mask0 = sb("mask0S", [128, CH], F32)
        self.mask1 = sb("mask1S", [128, CH], F32)
        self.halo_mask = sb("halo_maskS", [128, 1], F32)
        self.hsave = sb("hsave", [128, NIC * 3], F32)
        if self.which == "L2":
            self.nwattn = sb("nwattnS", [128, 32], F32)
            self.dcol128 = sb("dcol128S", [128, 64], F32)
            self.pmaskS = sb("pmaskSS", [128, 7], F32)
        phase = self.sb_off
        self.xn = sb("xn", [128, NIC * TH], BF16)
        self.ag = sb("ag", [128, NIC * TH], BF16)
        ffn_top = self.sb_off
        self.sb_off = phase
        self.xT = sb("xT", [128, 2 * DIN], BF16)
        self.xn2 = sb("xn2", [128, NIC * 259], BF16)
        self.prev = sb("prevS", [128, DIN], F32)
        mark = self.sb_off
        self.ynfm = sb("ynfm", [128, 32 * CH], BF16)
        ynfm_end = self.sb_off
        self.sb_off = mark
        self.pRscr = sb("pRscr", [128, DIN], F32)
        self.sb_off = max(self.sb_off, ynfm_end)
        self.Bfm = sb("BfmS", [128, CH], BF16)
        self.Cfm = sb("CfmS", [128, CH], BF16)
        self.Btok = sb("BtokS", [128, 2 * 128], BF16)
        self.dt_hm = sb("dt_hm", [64, CH], F32)
        self.dtv_hm = sb("dtv_hm", [64, CH], F32)
        self.dAc_hm = sb("dAc_hm", [64, CH], F32)
        self.tmp_hm = sb("tmp_hm", [64, CH], F32)
        self.dAcT = sb("dAcT", [128, 2 * 64], F32)
        self.dtT = sb("dtT", [128, 2 * 64], F32)
        self.dtvT = sb("dtvT", [128, 2 * 64], F32)
        self.cdc = sb("cdcS", [128, 64], F32)
        self.cd = sb("cdS", [64, 1], F32)
        self.cdp = sb("cdpS", [64, 1], F32)
        self.smallT = sb("smallTS", [64, 1], F32)
        self.Wp = sb("WpS", [128, 2 * CH], BF16)
        self.Mp = sb("MpS", [128, 2 * CH], BF16)
        self.convbf = self.nc.alloc_sbuf_tensor_at(
            "convbfS", [128, CH], BF16, offset=self.sb_off - 2 * CH * 2)
        self.jitw = sb("jitwS", [128, 192], BF16)
        self.jitD = sb("jitDS", [128, 128], BF16)
        self.Mpb = [self.Mp, self.Mp2]
        self.jitwb = [self.jitw, self.jitw2]
        self.jitDb = [self.jitD, self.jitD2]
        self.convs = sb("convsS", [128, 259], F32)
        self.convs2 = sb("convs2S", [128, 259], F32)
        self.ysc = sb("yscS", [128, CH], F32)
        mamba_top = self.sb_off
        assert max(ffn_top, mamba_top) <= SBUF_TOP, (ffn_top, mamba_top)

    def wait(self, q, sem, val):
        if val is None:
            return
        self.q[q].append(lambda eng, s=sem.h, v=val: eng.wait_ge(s, v))

    def op(self, q, build, inc=None, waits=()):
        if q == "gpsimd_c":
            q = "gpsimd"
            sem = self.sGC
            if sem.n > 0:
                self.wait(q, sem, sem.n)
            if inc is None:
                inc = (sem, 1)
        elif q in ("vector", "scalar"):
            sem = self.sV if q == "vector" else self.sA
            if sem.n > 0:
                self.wait(q, sem, sem.n)
            if inc is None:
                inc = (sem, 1)
        for w in waits:
            if w is not None:
                self.wait(q, w[0], w[1])
        tgt = None
        if inc is not None:
            inc[0].n += inc[1]
            tgt = inc[0].n

        def fn(eng, build=build, inc=inc):
            ins = build(eng)
            if inc is not None:
                ins.then_inc(inc[0].h, inc[1])

        self.q[q].append(fn)
        return tgt

    def guard(self, bank):
        g = self.bankfree[bank]
        return list(g) if g is not None else []

    def setg(self, bank, sem, val, extra=None):
        g = [(sem, val)]
        if extra is not None:
            g.append(extra)
        self.bankfree[bank] = g

    def wload(self, src):
        k = self.wload_n
        self.wload_n += 1
        buf = self.wbuf[k % 2]
        sem = self.sWL0 if k % 2 == 0 else self.sWL1
        a, b = src.shape[1], src.shape[2]
        dst = buf[:, 0:a * b].rearrange("p (a b) -> p a b", a=a)
        waits = []
        if k >= 2:
            waits.append((self.sWF, k - 1))
        tgt = self.op("gpsimd",
                      lambda eng, d=dst, s=src: eng.dma_start(out=d, in_=s),
                      inc=(sem, 16), waits=waits)
        return buf, (sem, tgt)

    def emit_setup(self):
        d, sIO = self.d, self.sIO
        loads = [
            ("gwt", self.gwt, "gpsimd"),
            ("eye_bf", self.eye_bf, "gpsimd"),
            ("b1sb", self.b1sb, "sync"), ("b2sb", self.b2sb, "sync"),
            ("gateb", self.gateb, "sync"), ("convw", self.convw, "sync"),
            ("convb", self.convb, "sync"), ("hsmall", self.hsmall, "sync"),
            ("nwffn", self.nwffn, "sync"),
            ("mask0", self.mask0, "sync"), ("mask1", self.mask1, "sync"),
            ("eye_f", self.eye_f, "sync"), ("halo_mask", self.halo_mask, "sync"),
        ]
        if self.which == "L2":
            loads += [("nwattn", self.nwattn, "sync"),
                      ("dcol128", self.dcol128, "sync"),
                      ("pmask", self.pmaskS, "sync")]
        for name, dst, qq in loads:
            self.op(qq, lambda eng, a=d[name], b=dst: eng.dma_start(out=b[:, :], in_=a),
                    inc=(sIO, 16) if qq == "sync" else (self.sG, 16))
        self.gload_ready = (self.sG, self.sG.n)
        src = d["uT"] if self.which == "L1" else d["x1in"]
        self.op("sync", lambda eng, a=src: eng.dma_start(out=self.X[:, :], in_=a),
                inc=(sIO, 16))
        self.x_ready = (sIO, sIO.n)
        self.consts = (sIO, sIO.n)
        self.op("vector", lambda eng: eng.memset(self.ones_f[:, :], 1.0))
        self.op("vector", lambda eng: eng.memset(self.onesc_f[:, :], 1.0))
        self.op("vector", lambda eng: eng.memset(self.onesrow[:, :], 1.0))
        self.op("vector", lambda eng: eng.memset(self.epsc[:, :], EPS))
        self.op("scalar", lambda eng: eng.activation(self.negexpA[:, :],
                                                     self.hsmall[:, 1:2], AF.Exp),
                waits=[self.consts], inc=(self.sA, 1))
        self.op("vector", lambda eng: eng.tensor_scalar(
            self.negexpA[:, :], self.negexpA[:, :], -1.0, None, ALU.mult),
            waits=[(self.sA, self.sA.n)], inc=(self.sV, 1))
        self.setup_done = (self.sV, self.sV.n)

    def emit_ffn(self, grps, x_ready, stages="all"):
        sT, sV, sA = self.sT, self.sV, self.sA
        pb = self.pb
        X, xn, ag = self.X, self.xn, self.ag
        first_t = [x_ready, self.setup_done]
        if self.sGC.n > 0:
            self.wait("vector", self.sGC, self.sGC.n)

        lo = min(c0 for c0, w in grps)
        hi = max(c0 + w for c0, w in grps)
        tiles = []
        o = lo
        while o < hi:
            tiles.append((o, min(128, hi - o)))
            o += 128
        S2 = self.scr2
        tile_done = [None] * len(tiles)
        emitted_tiles = 0
        for (c0, w) in grps:
            sq_done = []
            for ic in range(NIC):
                slot = self.scr1[:, 0:w] if ic % 2 == 0 else self.scr1[:, 514:514 + w]
                wts = [x_ready] if ic < 2 else [(sT, sq_done[ic - 2][1])]
                a_t = self.op("scalar",
                              lambda eng, s=slot, i=ic, c0=c0, w=w: eng.activation(
                                  s, X[:, i * TH + c0:i * TH + c0 + w], AF.Square),
                              waits=wts + ([x_ready] if ic < 2 else []),
                              inc=(sA, 1))
                g6 = self.guard(6) if ic == 0 else []
                self.op("tensor",
                        lambda eng, i=ic, c0=c0, w=w: eng.matmul(
                            pb[6][0:1, 0:w], self.onesc_f[:, :],
                            X[:, i * TH + c0:i * TH + c0 + w],
                            start=(i == 0), stop=(i == NIC - 1)),
                        waits=(first_t if ic == 0 else []) + g6)
                g7 = self.guard(7) if ic == 0 else []
                t = self.op("tensor",
                            lambda eng, s=slot, i=ic, w=w: eng.matmul(
                                pb[7][0:1, 0:w], self.onesc_f[:, :], s,
                                start=(i == 0), stop=(i == NIC - 1)),
                            waits=[(sA, a_t)] + g7,
                            inc=(sT, 1))
                sq_done.append((a_t, t))
            R = self.rows
            mvw = [(sT, sq_done[-1][1])]
            self.op("vector", lambda eng, w=w: eng.tensor_scalar(
                R[0:1, 0:w], pb[6][0:1, 0:w], 1.0 / 2048, None, ALU.mult), waits=mvw)
            self.op("vector", lambda eng, w=w: eng.tensor_scalar(
                R[0:1, 1024:1024 + w], pb[7][0:1, 0:w], 1.0 / 2048, None, ALU.mult))
            self.op("vector", lambda eng, w=w: eng.tensor_tensor(
                R[0:1, 512:512 + w], R[0:1, 0:w], R[0:1, 0:w], ALU.mult))
            self.op("vector", lambda eng, w=w: eng.tensor_tensor(
                R[0:1, 512:512 + w], R[0:1, 1024:1024 + w], R[0:1, 512:512 + w],
                ALU.subtract))
            self.op("vector", lambda eng, w=w: eng.tensor_scalar(
                R[0:1, 1024:1024 + w], R[0:1, 512:512 + w], EPS, None, ALU.add))
            self.op("vector", lambda eng, w=w: eng.reciprocal(
                R[0:1, 1536:1536 + w], R[0:1, 1024:1024 + w]))
            v1 = self.op("vector", lambda eng, w=w: eng.tensor_tensor(
                R[0:1, 1536:1536 + w], R[0:1, 512:512 + w], R[0:1, 1536:1536 + w],
                ALU.mult), inc=(sV, 1))
            self.op("scalar", lambda eng, w=w: eng.activation(
                R[0:1, 1024:1024 + w], R[0:1, 1024:1024 + w], AF.Ln),
                waits=[(sV, v1)])
            a1 = self.op("scalar", lambda eng, w=w: eng.activation(
                R[0:1, 1536:1536 + w], R[0:1, 1536:1536 + w], AF.Ln,
                bias=self.epsc[0:1, 0:1]), inc=(sA, 1))
            v2 = self.op("vector", lambda eng, w=w: eng.tensor_tensor(
                R[0:1, 512:512 + w], R[0:1, 1024:1024 + w], R[0:1, 1536:1536 + w],
                ALU.add), waits=[(sA, a1)], inc=(sV, 1))
            a2 = self.op("scalar", lambda eng, w=w: eng.activation(
                R[0:1, 512:512 + w], R[0:1, 512:512 + w], AF.Exp, scale=-0.5),
                waits=[(sV, v2)], inc=(sA, 1))
            ab1 = self.op("scalar", lambda eng, w=w: eng.activation(
                R[0:1, 1536:1536 + w], R[0:1, 1024:1024 + w], AF.Exp, scale=-0.5),
                inc=(sA, 1))
            v3 = self.op("vector", lambda eng, w=w: eng.tensor_tensor(
                R[0:1, 1024:1024 + w], R[0:1, 0:w], R[0:1, 512:512 + w], ALU.mult),
                waits=[(sA, ab1)], inc=(sV, 1))
            self.op("tensor", lambda eng, w=w: eng.matmul(
                pb[6][:, 0:w], self.ones_f[:, :], R[0:1, 512:512 + w],
                start=True, stop=True), waits=[(sV, v3)])
            t2 = self.op("tensor", lambda eng, w=w: eng.matmul(
                pb[7][:, 0:w], self.ones_f[:, :], R[0:1, 1024:1024 + w],
                start=True, stop=True), inc=(sT, 1))
            self.op("vector", lambda eng, w=w: eng.tensor_scalar(
                R[0:1, 1536:1536 + w], R[0:1, 1536:1536 + w], 0.5, None, ALU.mult))
            self.op("vector", lambda eng, w=w: eng.tensor_tensor(
                R[0:1, 512:512 + w], R[0:1, 1536:1536 + w], R[0:1, 0:w], ALU.mult),
                waits=[(sT, t2)])
            self.op("vector", lambda eng, w=w: eng.tensor_scalar(
                R[0:1, 1536:1536 + w], R[0:1, 1536:1536 + w], 1.0, None, ALU.add),
                inc=(sV, 1))
            lastv = None
            for ic in range(NIC):
                self.op("vector", lambda eng, i=ic, c0=c0, w=w: eng.tensor_tensor(
                    self.scr2[:, 0:w], X[:, i * TH + c0:i * TH + c0 + w],
                    pb[6][:, 0:w], ALU.mult),
                    waits=[(sT, t2)] if ic == 0 else [])
                self.op("vector", lambda eng, i=ic, w=w: eng.tensor_tensor(
                    self.scr2[:, 514:514 + w], self.scr2[:, 0:w], pb[7][:, 0:w],
                    ALU.subtract))
                lastv = self.op("vector", lambda eng, i=ic, c0=c0, w=w: eng.tensor_scalar(
                    xn[:, i * TH + c0:i * TH + c0 + w], self.scr2[:, 514:514 + w],
                    self.nwffn[:, i:i + 1], None, ALU.mult), inc=(sV, 1))
            self.op("tensor", lambda eng, w=w: eng.matmul(
                pb[6][:, 0:w], self.ones_f[:, :], R[0:1, 1536:1536 + w],
                start=True, stop=True), waits=[(sV, lastv)])
            tb2 = self.op("tensor", lambda eng, w=w: eng.matmul(
                pb[7][:, 0:w], self.ones_f[:, :], R[0:1, 512:512 + w],
                start=True, stop=True), inc=(sT, 1))
            lastr = None
            for ic in range(NIC):
                self.op("vector", lambda eng, i=ic, c0=c0, w=w: eng.tensor_tensor(
                    self.scr2[:, 0:w], X[:, i * TH + c0:i * TH + c0 + w],
                    pb[6][:, 0:w], ALU.mult),
                    waits=[(sT, tb2)] if ic == 0 else [])
                lastr = self.op("vector", lambda eng, i=ic, c0=c0, w=w: eng.tensor_tensor(
                    X[:, i * TH + c0:i * TH + c0 + w], self.scr2[:, 0:w],
                    pb[7][:, 0:w], ALU.subtract), inc=(sV, 1))
            self.setg(6, sV, lastr)
            self.setg(7, sV, lastr)


            if stages == "ln":
                continue
            for ic in range(NIC):
                self.op("tensor", lambda eng, i=ic, c0=c0, w=w: eng.matmul(
                    pb[6][0:16, 0:w], self.gwt[:, i * 16:(i + 1) * 16],
                    xn[:, i * TH + c0:i * TH + c0 + w],
                    start=(i == 0), stop=False),
                    waits=(self.guard(6) + [self.gload_ready]) if ic == 0 else [])
            tg = self.op("tensor", lambda eng, w=w: eng.matmul(
                pb[6][0:16, 0:w], self.gateb[:, :], self.onesrow[0:1, 0:w],
                start=False, stop=True), inc=(sT, 1))
            ve = self.op("vector", lambda eng, c0=c0, w=w: eng.tensor_copy(
                self.logit[:, c0:c0 + w], pb[6][0:16, 0:w]),
                waits=[(sT, tg)], inc=(sV, 1))
            self.setg(6, sV, ve)
            logit_done = ve

            cover = c0 + w
            while (emitted_tiles < len(tiles) and
                   tiles[emitted_tiles][0] + tiles[emitted_tiles][1] <= cover):
                ti = emitted_tiles
                o, tw = tiles[ti]
                tt = self.op("tensor", lambda eng, o=o, tw=tw: eng.transpose(
                    pb[7][0:tw, 0:16], self.logit[:, o:o + tw],
                    self.eye_f[0:16, 0:16]),
                    waits=self.guard(7) + [(sV, logit_done)],
                    inc=(sT, 1))
                c = self.op("vector", lambda eng, tw=tw: eng.tensor_copy(
                    S2[0:tw, 0:16], pb[7][0:tw, 0:16]), waits=[(sT, tt)], inc=(sV, 1))
                self.setg(7, sV, c)
                self.op("vector", lambda eng, tw=tw: eng.reduce_max(
                    S2[0:tw, 16:17], S2[0:tw, 0:16], AX.X, negate=True))
                vx = self.op("vector", lambda eng, tw=tw: eng.memset(
                    S2[0:tw, 48:64], 0.0), inc=(sV, 1))
                ae = self.op("scalar", lambda eng, tw=tw: eng.activation(
                    S2[0:tw, 32:48], S2[0:tw, 0:16], AF.Exp, bias=S2[0:tw, 16:17]),
                    waits=[(sV, vx)], inc=(sA, 1))
                for j in range(16):
                    self.op("vector", lambda eng, j=j, tw=tw: eng.tensor_scalar(
                        S2[0:tw, 64:80], S2[0:tw, 0:16], S2[0:tw, j:j + 1], None,
                        ALU.is_lt))
                    self.op("vector", lambda eng, tw=tw: eng.tensor_tensor(
                        S2[0:tw, 48:64], S2[0:tw, 48:64], S2[0:tw, 64:80], ALU.add))
                self.op("vector", lambda eng, tw=tw: eng.tensor_scalar(
                    S2[0:tw, 48:64], S2[0:tw, 48:64], 4.0, None, ALU.is_lt))
                self.op("vector", lambda eng, tw=tw: eng.tensor_tensor(
                    S2[0:tw, 48:64], S2[0:tw, 48:64], S2[0:tw, 32:48], ALU.mult),
                    waits=[(sA, ae)])
                self.op("vector", lambda eng, tw=tw: eng.reduce_sum(
                    S2[0:tw, 16:17], S2[0:tw, 48:64], AX.X))
                self.op("vector", lambda eng, tw=tw: eng.reciprocal(
                    S2[0:tw, 17:18], S2[0:tw, 16:17]))
                gv = self.op("vector", lambda eng, ti=ti, tw=tw: eng.tensor_scalar(
                    self.Gtt[0:tw, ti * 16:(ti + 1) * 16], S2[0:tw, 48:64],
                    S2[0:tw, 17:18], None, ALU.mult), inc=(sV, 1))
                tb = self.op("tensor", lambda eng, ti=ti, o=o, tw=tw: eng.transpose(
                    pb[7][0:16, 0:tw], self.Gtt[0:tw, ti * 16:(ti + 1) * 16],
                    self.eye_f[0:tw, 0:tw]),
                    waits=[(sV, gv)] + self.guard(7), inc=(sT, 1))
                c2 = self.op("vector", lambda eng, o=o, tw=tw: eng.tensor_scalar(
                    self.Gfm_a[0:16, o:o + tw], pb[7][0:16, 0:tw],
                    0.5 / 1.702, None, ALU.mult), waits=[(sT, tb)])
                c3 = self.op("vector", lambda eng, o=o, tw=tw: eng.tensor_scalar(
                    self.Gfm_b[0:16, o:o + tw], pb[7][0:16, 0:tw],
                    0.5, None, ALU.mult), inc=(sV, 1))
                self.setg(7, sV, c3)


                tile_done[ti] = c3
                emitted_tiles += 1
        if stages == "ln":
            return (sV, sV.n)
        g_ready = []
        for (c0, w) in grps:
            mx = 0
            for ti, (o, tw) in enumerate(tiles):
                if o < c0 + w and o + tw > c0:
                    mx = max(mx, tile_done[ti])
            g_ready.append(mx)
        g_done = (sV, sV.n)
        if stages == "gate":
            return (sV, sV.n)
        for e in range(self.nexp):
            for gi, (c0, w) in enumerate(grps):
                gm = self.op("tensor", lambda eng, e=e, c0=c0, w=w: eng.matmul(
                    pb[6][0:1, 0:w], self.eye_f[32:48, 32 + e:33 + e],
                    self.Gfm_a[:, c0:c0 + w], start=True, stop=True),
                    waits=self.guard(6) + ([(sV, g_ready[gi])] if e == 0 else []),
                    inc=(sT, 1))
                gv0 = self.op("vector", lambda eng, w=w: eng.tensor_copy(
                    self.rows[0:1, 1024:1024 + w], pb[6][0:1, 0:w]),
                    waits=[(sT, gm)], inc=(sV, 1))
                gm2 = self.op("tensor", lambda eng, w=w: eng.matmul(
                    pb[6][:, 0:w], self.ones_f[:, :], self.rows[0:1, 1024:1024 + w],
                    start=True, stop=True), waits=[(sV, gv0)], inc=(sT, 1))
                gv = self.op("vector", lambda eng, c0=c0, w=w: eng.tensor_copy(
                    self.Gb[:, c0:c0 + w], pb[6][:, 0:w]),
                    waits=[(sT, gm2)], inc=(sV, 1))
                self.setg(6, sV, gv)
            unit = 0
            for pr in range(NEXP):
                og = pr // 2
                if pr % 2 == 0:
                    wb, wrdy = self.wload(
                        self.d["w1tp"][e * 2048:(e + 1) * 2048,
                                       og * 512:(og + 1) * 512]
                        .rearrange("(ic p) o -> p ic o", p=128))
                for gi, (c0, w) in enumerate(grps):
                    bA, bB = unit % 2, 2 + unit % 2
                    wts = []
                    if gi == 0 and pr % 2 == 0:
                        wts.append(wrdy)
                    wts += self.guard(bA)
                    for ic in range(NIC):
                        self.op("tensor", lambda eng, ic=ic, pr=pr, c0=c0, w=w,
                                wb=wb, bA=bA: eng.matmul(
                                    pb[bA][:, 0:w],
                                    wb[:, ic * 512 + (pr % 2) * 256:
                                       ic * 512 + (pr % 2) * 256 + 128],
                                    xn[:, ic * TH + c0:ic * TH + c0 + w],
                                    start=(ic == 0), stop=(ic == NIC - 1)),
                                waits=wts if ic == 0 else [])
                    wts2 = self.guard(bB)
                    for ic in range(NIC):
                        tB = self.op("tensor", lambda eng, ic=ic, pr=pr, c0=c0, w=w,
                                     wb=wb, bB=bB: eng.matmul(
                                         pb[bB][:, 0:w],
                                         wb[:, ic * 512 + (pr % 2) * 256 + 128:
                                            ic * 512 + (pr % 2) * 256 + 256],
                                         xn[:, ic * TH + c0:ic * TH + c0 + w],
                                         start=(ic == 0), stop=(ic == NIC - 1)),
                                     waits=wts2 if ic == 0 else [],
                                     inc=(sT, 1) if ic == NIC - 1 else None)
                    last_w1 = (pr % 2 == 1 and gi == len(grps) - 1)
                    b1a = self.b1sb[:, e * 32 + 2 * pr:e * 32 + 2 * pr + 1]
                    b1b = self.b1sb[:, e * 32 + 2 * pr + 1:e * 32 + 2 * pr + 2]
                    self.op("vector", lambda eng, bA=bA, w=w, b1a=b1a: eng.tensor_scalar(
                        self.scr1[:, 0:w], pb[bA][:, 0:w], b1a, 7.0, ALU.add, ALU.min),
                        waits=[(sT, tB)])
                    if last_w1:
                        self._dummy_wfree()
                    uv = self.op("vector", lambda eng, bB=bB, w=w, b1b=b1b: eng.tensor_scalar(
                        self.scr2[:, 0:w], pb[bB][:, 0:w], b1b, 7.0, ALU.add, ALU.min),
                        inc=(sV, 1))
                    self.setg(bA, sV, uv)
                    self.setg(bB, sV, uv)
                    self.op("vector", lambda eng, w=w: eng.tensor_scalar(
                        self.scr2[:, 0:w], self.scr2[:, 0:w], -7.0, 1.0,
                        ALU.max, ALU.add))
                    at = self.op("scalar", lambda eng, w=w: eng.activation(
                        self.scr1[:, 514:514 + w], self.scr1[:, 0:w], AF.Silu,
                        scale=1.702), waits=[(sV, uv)], inc=(sA, 1))
                    self.op("vector", lambda eng, w=w: eng.tensor_tensor(
                        self.scr2[:, 514:514 + w], self.scr1[:, 514:514 + w],
                        self.scr2[:, 0:w], ALU.mult), waits=[(sA, at)])
                    self.op("vector", lambda eng, pr=pr, c0=c0, w=w: eng.tensor_tensor(
                        ag[:, pr * TH + c0:pr * TH + c0 + w],
                        self.scr2[:, 514:514 + w], self.Gb[:, c0:c0 + w], ALU.mult),
                        inc=(sV, 1))
                    unit += 1
            ag_done = (sV, sV.n)
            u2 = 0
            for wg in range(4):
                wb2, wrdy2 = self.wload(
                    self.d["w2t"][e * 2048:(e + 1) * 2048, wg * 512:(wg + 1) * 512]
                    .rearrange("(ic p) o -> p ic o", p=128))
                for dsub in range(4):
                    dt_ = wg * 4 + dsub
                    for gi, (c0, w) in enumerate(grps):
                        bk = 4 + u2 % 2
                        wts = list(self.guard(bk))
                        if dsub == 0 and gi == 0:
                            wts.append(wrdy2)
                        if wg == 0 and dsub == 0 and gi == 0:
                            wts.append(ag_done)
                        tl = None
                        for ic in range(NIC):
                            is_stop = (ic == NIC - 1) and e != self.nexp - 1
                            tl = self.op("tensor", lambda eng, ic=ic, dsub=dsub, c0=c0,
                                         w=w, wb2=wb2, bk=bk, st=is_stop: eng.matmul(
                                             pb[bk][:, 0:w],
                                             wb2[:, ic * 512 + dsub * 128:
                                                 ic * 512 + dsub * 128 + 128],
                                             ag[:, ic * TH + c0:ic * TH + c0 + w],
                                             start=(ic == 0), stop=st),
                                         waits=wts if ic == 0 else [],
                                         inc=(sT, 1) if (ic == NIC - 1 and e != 15) else None)
                        if e == self.nexp - 1:
                            tl = self.op("tensor", lambda eng, dt_=dt_, c0=c0, w=w,
                                         bk=bk: eng.matmul(
                                             pb[bk][:, 0:w],
                                             self.b2sb[:, dt_ * 128:(dt_ + 1) * 128],
                                             self.Gfm_b[:, c0:c0 + w],
                                             start=False, stop=True),
                                         inc=(sT, 1))
                        last_w2 = (dsub == 3 and gi == len(grps) - 1)
                        ev = self.op("vector", lambda eng, dt_=dt_, c0=c0, w=w,
                                     bk=bk: eng.tensor_tensor(
                                         X[:, dt_ * TH + c0:dt_ * TH + c0 + w],
                                         X[:, dt_ * TH + c0:dt_ * TH + c0 + w],
                                         pb[bk][:, 0:w], ALU.add),
                                     waits=[(sT, tl)], inc=(sV, 1))
                        self.setg(bk, sV, ev)
                        if last_w2:
                            self._dummy_wfree()
                        u2 += 1
        return (sV, sV.n)

    def _dummy_wfree(self, waits=()):
        w = list(waits)
        if self.sWF.n > 0:
            w.append((self.sWF, self.sWF.n))
        self.op("vector", lambda eng: eng.tensor_copy(
            self.dumc[0:1, 0:1], self.epsc[0:1, 0:1]),
            inc=(self.sWF, 1), waits=w)

    def emit_mamba(self, full, ffn_done):
        for q in range(NCHUNK):
            self._chunk(q, full, ffn_done if q == 0 else None)
        return (self.sV, self.sV.n)

    def _ln2(self, q, first_wait):
        sT, sV, sA = self.sT, self.sV, self.sA
        pb, X, R = self.pb, self.X, self.rows
        base, w = q * CH, 259
        sq_done = []
        for ic in range(NIC):
            slot = self.scr1[:, 0:w] if ic % 2 == 0 else self.scr1[:, 514:514 + w]
            wts = [] if ic >= 2 else ([first_wait] if first_wait else [])
            if ic >= 2:
                wts = [(sT, sq_done[ic - 2])]
            a_t = self.op("scalar", lambda eng, s=slot, i=ic: eng.activation(
                s, X[:, i * TH + base:i * TH + base + w], AF.Square),
                waits=wts, inc=(sA, 1))
            self.op("tensor", lambda eng, i=ic: eng.matmul(
                pb[6][0:1, 0:w], self.onesc_f[:, :],
                X[:, i * TH + base:i * TH + base + w],
                start=(i == 0), stop=(i == NIC - 1)),
                waits=(self.guard(6) + [self.setup_done, self.gload_ready]
                       + ([first_wait] if first_wait else [])) if ic == 0 else [])
            t = self.op("tensor", lambda eng, s=slot, i=ic: eng.matmul(
                pb[7][0:1, 0:w], self.onesc_f[:, :], s,
                start=(i == 0), stop=(i == NIC - 1)),
                waits=[(sA, a_t)] + (self.guard(7) if ic == 0 else []),
                inc=(sT, 1))
            sq_done.append(t)
        self.op("vector", lambda eng: eng.tensor_scalar(
            R[0:1, 0:w], pb[6][0:1, 0:w], 1.0 / 2048, None, ALU.mult),
            waits=[(sT, sq_done[-1])])
        self.op("vector", lambda eng: eng.tensor_scalar(
            R[0:1, 1024:1024 + w], pb[7][0:1, 0:w], 1.0 / 2048, None, ALU.mult))
        self.op("vector", lambda eng: eng.tensor_tensor(
            R[0:1, 512:512 + w], R[0:1, 0:w], R[0:1, 0:w], ALU.mult))
        v1 = self.op("vector", lambda eng: eng.tensor_tensor(
            R[0:1, 512:512 + w], R[0:1, 1024:1024 + w], R[0:1, 512:512 + w],
            ALU.subtract), inc=(sV, 1))
        a1 = self.op("scalar", lambda eng: eng.activation(
            R[0:1, 512:512 + w], R[0:1, 512:512 + w], AF.Ln,
            bias=self.epsc[0:1, 0:1]), waits=[(sV, v1)], inc=(sA, 1))
        a2 = self.op("scalar", lambda eng: eng.activation(
            R[0:1, 512:512 + w], R[0:1, 512:512 + w], AF.Exp, scale=-0.5),
            inc=(sA, 1))
        v3 = self.op("vector", lambda eng: eng.tensor_tensor(
            R[0:1, 1024:1024 + w], R[0:1, 0:w], R[0:1, 512:512 + w], ALU.mult),
            waits=[(sA, a2)], inc=(sV, 1))
        self.op("tensor", lambda eng: eng.matmul(
            pb[6][:, 0:w], self.ones_f[:, :], R[0:1, 512:512 + w],
            start=True, stop=True), waits=[(sV, v3)])
        t2 = self.op("tensor", lambda eng: eng.matmul(
            pb[7][:, 0:w], self.ones_f[:, :], R[0:1, 1024:1024 + w],
            start=True, stop=True), inc=(sT, 1))
        lastv = None
        for ic in range(NIC):
            self.op("vector", lambda eng, i=ic: eng.tensor_tensor(
                self.scr2[:, 0:w], X[:, i * TH + base:i * TH + base + w],
                pb[6][:, 0:w], ALU.mult), waits=[(sT, t2)] if ic == 0 else [])
            lastv = self.op("vector", lambda eng, i=ic: eng.tensor_tensor(
                self.xn2[:, i * 259:i * 259 + w], self.scr2[:, 0:w],
                pb[7][:, 0:w], ALU.subtract), inc=(sV, 1))
        self.setg(6, sV, lastv)
        self.setg(7, sV, lastv)
        if q == 0:
            lastv = self.op("vector", lambda eng: eng.tensor_scalar(
                self.xn2[:, :].rearrange("p (ic t) -> p ic t", ic=NIC)[:, :, 0:3],
                self.xn2[:, :].rearrange("p (ic t) -> p ic t", ic=NIC)[:, :, 0:3],
                self.halo_mask[:, 0:1], None, ALU.mult), inc=(sV, 1))
        return lastv, t2

    def _dt_math(self, q, bk, dt_psum_done):
        sT, sV, sA = self.sT, self.sV, self.sA
        pb = self.pb
        drain = []
        if self._mp_pairs > 0:
            drain = [(self.sFD[0], self.sFD[0].n), (self.sFD[1], self.sFD[1].n)]
        v0 = self.op("vector", lambda eng, bk=bk: eng.tensor_scalar(
            self.dt_hm[:, :], pb[bk][0:64, 0:256], self.hsmall[:, 0:1], None,
            ALU.add), waits=[(sT, dt_psum_done)] + drain, inc=(sV, 1))
        self.setg(bk, sV, v0)
        self._dummy_wfree()
        self.op("scalar", lambda eng: eng.activation(
            self.tmp_hm[:, :], self.dt_hm[:, :], AF.Abs), waits=[(sV, v0)])
        self.op("scalar", lambda eng: eng.activation(
            self.tmp_hm[:, :], self.tmp_hm[:, :], AF.Exp, scale=-1.0))
        a1 = self.op("scalar", lambda eng: eng.activation(
            self.tmp_hm[:, :], self.tmp_hm[:, :], AF.Ln,
            bias=self.onesc_f[0:64, 0:1]), inc=(sA, 1))
        self.op("vector", lambda eng: eng.tensor_scalar(
            self.dtv_hm[:, :], self.dt_hm[:, :], 0.0, None, ALU.max))
        self.op("vector", lambda eng: eng.tensor_tensor(
            self.dt_hm[:, :], self.dtv_hm[:, :], self.tmp_hm[:, :], ALU.add),
            waits=[(sA, a1)])
        self.op("vector", lambda eng: eng.tensor_scalar(
            self.tmp_hm[:, :], self.dt_hm[:, :], self.negexpA[0:64, 0:1], None,
            ALU.mult))
        v1 = self.op("vector", lambda eng: eng.tensor_tensor_scan(
            self.dAc_hm[:, :], self.tmp_hm[:, :], self.tmp_hm[:, :], 0.0,
            ALU.add, ALU.bypass), inc=(sV, 1))
        a2 = self.op("scalar", lambda eng: eng.activation(
            self.cd[:, :], self.dAc_hm[:, 255:256], AF.Exp),
            waits=[(sV, v1)], inc=(sA, 1))
        if q == 0:
            self.op("vector", lambda eng: eng.tensor_copy(
                self.cdp[:, :], self.cd[:, :]), waits=[(sA, a2)])
        else:
            self.op("vector", lambda eng: eng.tensor_tensor(
                self.cdp[:, :], self.cdp[:, :], self.cd[:, :], ALU.mult),
                waits=[(sA, a2)])
        v2 = self.op("vector", lambda eng: eng.tensor_scalar(
            self.tmp_hm[:, :], self.dAc_hm[:, :], self.dAc_hm[:, 255:256], None,
            ALU.subtract), inc=(sV, 1))
        a3 = self.op("scalar", lambda eng: eng.activation(
            self.tmp_hm[:, :], self.tmp_hm[:, :], AF.Exp, scale=-1.0),
            waits=[(sV, v2)], inc=(sA, 1))
        vdtv = self.op("vector", lambda eng: eng.tensor_tensor(
            self.dtv_hm[:, :], self.dt_hm[:, :], self.tmp_hm[:, :], ALU.mult),
            waits=[(sA, a3)], inc=(sV, 1))
        if self.which == "L2":
            ae = self.op("scalar", lambda eng: eng.activation(
                self.tmp_hm[:, :], self.dAc_hm[:, :], AF.Exp),
                waits=[(sV, vdtv)], inc=(sA, 1))
            self._ehm_ready = (sA, ae)
        for (srcT, dstT, wv) in ((self.dAc_hm, self.dAcT, v1),
                                 (self.dt_hm, self.dtT, v1),
                                 (self.dtv_hm, self.dtvT, vdtv)):
            for st in range(2):
                tt = self.op("tensor", lambda eng, s=srcT, st=st: eng.transpose(
                    pb[5][:, 0:64], s[0:64, st * 128:st * 128 + 128],
                    self.eye_f[0:64, 0:64]),
                    waits=[(sV, wv)] + self.guard(5), inc=(sT, 1))
                ve = self.op("vector", lambda eng, d=dstT, st=st: eng.tensor_copy(
                    d[:, st * 64:st * 64 + 64], pb[5][:, 0:64]),
                    waits=[(sT, tt)], inc=(sV, 1))
                self.setg(5, sV, ve)
        tt = self.op("tensor", lambda eng: eng.transpose(
            pb[5][0:1, 0:64], self.cd[:, 0:1], self.eye_f[0:64, 0:64]),
            waits=[(sA, a2)] + self.guard(5), inc=(sT, 1))
        ve = self.op("vector", lambda eng: eng.tensor_copy(
            self.rows[0:1, 1792:1856], pb[5][0:1, 0:64]),
            waits=[(sT, tt)], inc=(sV, 1))
        tt2 = self.op("tensor", lambda eng: eng.matmul(
            pb[5][:, 0:64], self.ones_f[:, :], self.rows[0:1, 1792:1856],
            start=True, stop=True), waits=[(sV, ve)], inc=(sT, 1))
        ve2 = self.op("vector", lambda eng: eng.tensor_copy(
            self.cdc[:, :], pb[5][:, 0:64]), waits=[(sT, tt2)], inc=(sV, 1))
        self.setg(5, sV, ve2)
        return ve2

    def _in_proj_conv(self, q, xn2_done):
        sT, sV, sA = self.sT, self.sV, self.sA
        pb = self.pb
        grp_list = [(4096 + 512 * k, 512) for k in range(8)] + [(8192, 320)]
        unit = 0
        xt_done = None
        dt_done = None
        for (gc0, gw) in grp_list:
            wb, wrdy = self.wload(
                self.d["iprojT"][:, gc0:gc0 + gw]
                .rearrange("(ic p) o -> p ic o", p=128))
            ntile = gw // 128 if gw % 128 == 0 else 3
            for j in range(ntile):
                ot = (gc0 - 4096) // 128 + j
                is_dt = (gc0 == 8192 and j == 2)
                tw = 64 if is_dt else 128
                bk = unit % 2
                wts = list(self.guard(bk))
                if j == 0:
                    wts.append(wrdy)
                    if xn2_done is not None and gc0 == 4096:
                        wts.append((sV, xn2_done))
                tlast = None
                for ic in range(NIC):
                    if is_dt:
                        tlast = self.op("tensor", lambda eng, ic=ic, wb=wb, gw=gw,
                                        j=j, bk=bk: eng.matmul(
                                            pb[bk][0:64, 0:256],
                                            wb[:, ic * gw + j * 128:ic * gw + j * 128 + 64],
                                            self.xn2[:, ic * 259 + 3:ic * 259 + 259],
                                            start=(ic == 0), stop=(ic == NIC - 1)),
                                        waits=wts if ic == 0 else [],
                                        inc=(sT, 1) if ic == NIC - 1 else None)
                    else:
                        tlast = self.op("tensor", lambda eng, ic=ic, wb=wb, gw=gw,
                                        j=j, bk=bk: eng.matmul(
                                            pb[bk][:, 0:259],
                                            wb[:, ic * gw + j * 128:ic * gw + j * 128 + 128],
                                            self.xn2[:, ic * 259:ic * 259 + 259],
                                            start=(ic == 0), stop=(ic == NIC - 1)),
                                        waits=wts if ic == 0 else [],
                                        inc=(sT, 1) if ic == NIC - 1 else None)
                if is_dt:
                    dt_done = self._dt_math(q, bk, tlast)
                    unit += 1
                    continue
                self.op("vector", lambda eng, bk=bk, ot=ot: eng.tensor_scalar(
                    self.convs[:, 0:256], pb[bk][:, 0:256],
                    self.convw[:, ot * 4:ot * 4 + 1], None, ALU.mult),
                    waits=[(sT, tlast)])
                self.op("vector", lambda eng, bk=bk, ot=ot: eng.scalar_tensor_tensor(
                    self.convs2[:, 0:256], pb[bk][:, 1:257],
                    self.convw[:, ot * 4 + 1:ot * 4 + 2], self.convs[:, 0:256],
                    ALU.mult, ALU.add))
                self.op("vector", lambda eng, bk=bk, ot=ot: eng.scalar_tensor_tensor(
                    self.convs[:, 0:256], pb[bk][:, 2:258],
                    self.convw[:, ot * 4 + 2:ot * 4 + 3], self.convs2[:, 0:256],
                    ALU.mult, ALU.add))
                cv = self.op("vector", lambda eng, bk=bk, ot=ot: eng.scalar_tensor_tensor(
                    self.convs2[:, 0:256], pb[bk][:, 3:259],
                    self.convw[:, ot * 4 + 3:ot * 4 + 4], self.convs[:, 0:256],
                    ALU.mult, ALU.add), inc=(sV, 1))
                self.setg(bk, sV, cv)
                if gc0 != 8192 and j == ntile - 1:
                    self._dummy_wfree()
                if ot < 32:
                    dst = self.convbf[:, 0:256]
                elif ot == 32:
                    dst = self.Bfm[:, :]
                else:
                    dst = self.Cfm[:, :]
                av = self.op("scalar", lambda eng, d=dst, ot=ot: eng.activation(
                    d, self.convs2[:, 0:256], AF.Silu,
                    bias=self.convb[:, ot:ot + 1]), waits=[(sV, cv)], inc=(sA, 1))
                self._conv_av = av
                if ot < 32 or ot == 32:
                    for st in range(2):
                        tt = self.op("tensor", lambda eng, ot=ot, st=st, d=dst:
                                     eng.transpose(
                                         self.pbf[5][:, st * 128:st * 128 + 128],
                                         d[:, st * 128:st * 128 + 128],
                                         self.eye_bf[:, :]),
                                     waits=[(sA, av)] + self.guard(5), inc=(sT, 1))
                        if ot < 32:
                            tgt = self.xT[:, st * 4096 + ot * 128:
                                          st * 4096 + ot * 128 + 128]
                        else:
                            tgt = self.Btok[:, st * 128:st * 128 + 128]
                        ve = self.op("vector", lambda eng, t=tgt, st=st: eng.tensor_copy(
                            t, self.pbf[5][:, st * 128:st * 128 + 128]),
                            waits=[(sT, tt)], inc=(sV, 1))
                        self.setg(5, sV, ve)
                        xt_done = ve
                unit += 1
        return xt_done, dt_done

    def _scan_states(self, q, xt_done, dt_done):
        sT, sV = self.sT, self.sV
        pb = self.pb
        last = None
        for hg in range(8):
            bk = hg % 2
            g = self.guard(bk)
            smm = None
            pend = None
            first_smm = True

            def emit_smm(pd):
                nonlocal smm, first_smm
                slot, jv, jw_, hp = pd
                for st in range(2):
                    smm = self.op("tensor", lambda eng, st=st, bk=bk, slot=slot,
                                  jw_=jw_: eng.matmul(
                                      pb[bk][:, slot:slot + 64],
                                      self.Btok[:, st * 128:st * 128 + 128],
                                      jw_[:, st * 64:st * 64 + 64],
                                      start=(st == 0), stop=(st == 1)),
                                  waits=[(self.sGC, jv)] +
                                  (g if (first_smm and st == 0) else []),
                                  inc=(sT, 1) if st == 1 else None)
                first_smm = False
                self._jf2[hp] = smm

            for h8 in range(8):
                h = hg * 8 + h8
                hp = h % 2
                jw_ = self.jitwb[hp]
                jfree = self._jf2[hp]
                jv = None
                for st in range(2):
                    w0 = []
                    if st == 0:
                        w0 = [(sT, jfree)]
                        if h == 0:
                            w0.append((sV, max(xt_done, dt_done)))
                    jv = self.op("gpsimd_c", lambda eng, h=h, st=st, jw_=jw_:
                                 eng.tensor_scalar(
                                     jw_[:, st * 64:st * 64 + 64],
                                     self.xT[:, st * 4096 + h * 64:st * 4096 + h * 64 + 64],
                                     self.dtvT[:, st * 64 + h:st * 64 + h + 1], None,
                                     ALU.mult),
                                 waits=w0)
                if pend is not None:
                    emit_smm(pend)
                pend = (h8 * 64, jv, jw_, hp)
            emit_smm(pend)
            for h8 in range(8):
                h = hg * 8 + h8
                slot = h8 * 64
                self.op("vector", lambda eng, h=h: eng.tensor_scalar(
                    self.prev[:, h * 64:h * 64 + 64], self.prev[:, h * 64:h * 64 + 64],
                    self.cdc[:, h:h + 1], None, ALU.mult),
                    waits=[(sT, smm)] if h8 == 0 else [])
                last = self.op("vector", lambda eng, h=h, bk=bk, slot=slot: eng.tensor_tensor(
                    self.prev[:, h * 64:h * 64 + 64], self.prev[:, h * 64:h * 64 + 64],
                    pb[bk][:, slot:slot + 64], ALU.add), inc=(sV, 1))
            self.setg(bk, sV, last)
        return last

    def _chunk(self, q, full, first_wait):
        sT, sV, sA = self.sT, self.sV, self.sA
        pb = self.pb
        base = q * CH
        if self.sGC.n > 0:
            self.wait("vector", self.sGC, self.sGC.n)
        xn2_done, ln_t2 = self._ln2(q, first_wait)
        if getattr(self, "_fix_req", False):
            self.wait("vector", sT, ln_t2)
            for ic in range(NIC):
                self.op("vector", lambda eng, ic=ic, q=q: eng.tensor_copy(
                    self.X[:, ic * TH + q * CH:ic * TH + q * CH + 3],
                    self.hsave[:, ic * 3:ic * 3 + 3]))
            self.op("vector", lambda eng: eng.tensor_copy(
                self.dumc[0:1, 0:1], self.epsc[0:1, 0:1]), inc=(self.sV, 1))
        xt_done, dt_done = self._in_proj_conv(q, xn2_done)
        if not full:
            self._scan_states(q, xt_done, dt_done)
            return
        wp_done = None
        for st in range(2):
            msk = self.mask0 if st == 0 else self.mask1
            tt = self.op("tensor", lambda eng, st=st: eng.matmul(
                pb[6][:, 0:CH], self.Bfm[:, st * 128:st * 128 + 128],
                self.Cfm[:, 0:CH], start=True, stop=True),
                waits=self.guard(6) + [(sV, xt_done), (sA, self._conv_av)],
                inc=(sT, 1))
            wp_done = self.op("vector", lambda eng, st=st, m=msk: eng.tensor_tensor(
                self.Wp[:, st * CH:(st + 1) * CH], pb[6][:, 0:CH], m[:, :],
                ALU.mult), waits=[(sT, tt)], inc=(sV, 1))
            self.setg(6, sV, wp_done)
        if getattr(self, "mstop", None) == "W":
            return
        self._prev_rdy = self.sV.n
        ssq_first = True
        last_sq = 0
        for hb in range(16):
            bD = 0 + hb % 2
            bO = 2 + hb % 2
            au_t = [0] * 4
            for hp2 in range(2):
                hh0 = hp2 * 2
                h0 = hb * 4 + hh0
                k = self._mp_pairs
                self._mp_pairs += 1
                slot = k % 2
                semD = self.sFD[slot]
                pbank = 6 if slot == 0 else 4
                offA = 1024 if slot == 0 else 0
                offB = 1536 if slot == 0 else 512
                war = self._mp_war[slot]
                dw = [(sV, wp_done), self._ehm_ready]
                if war is not None:
                    dw.append((sT, war))
                for j in range(2):
                    self.op("gpsimd", lambda eng, h=h0, j=j, offA=offA:
                            eng.dma_start(
                                out=self.rows[0:1, offA + j * CH:
                                              offA + (j + 1) * CH],
                                in_=self.dAc_hm[h + j:h + j + 1, 0:CH]),
                            inc=(semD, 16), waits=dw if j == 0 else [])
                for j in range(2):
                    self.op("gpsimd", lambda eng, h=h0, j=j, offB=offB:
                            eng.dma_start(
                                out=self.rows[0:1, offB + j * CH:
                                              offB + (j + 1) * CH],
                                in_=self.tmp_hm[h + j:h + j + 1, 0:CH]),
                            inc=(semD, 16))
                dtgt = semD.n
                bcp = self.op("tensor", lambda eng, offA=offA, pbank=pbank:
                              eng.matmul(
                                  pb[pbank][:, 0:512], self.ones_f[:, :],
                                  self.rows[0:1, offA:offA + 512],
                                  start=True, stop=True),
                              waits=self.guard(pbank) + [(semD, dtgt)],
                              inc=(sT, 1))
                pinfo = []
                dv = None
                for j in range(2):
                    hh = hh0 + j
                    h = hb * 4 + hh
                    hp = h % 2
                    Mp_ = self.Mpb[hp]
                    jfree = self._jf2[hp]
                    poff = j * CH
                    mpt = None
                    for st in range(2):
                        dw2 = [(sT, bcp)] if (st == 0 and j == 0) else []
                        dv = self.op("vector", lambda eng, h=h, st=st, poff=poff,
                                     pbank=pbank: eng.tensor_scalar(
                                         self.scr1[:, 0:CH],
                                         pb[pbank][:, poff:poff + CH],
                                         self.dAcT[:, st * 64 + h:st * 64 + h + 1],
                                         0.0, ALU.subtract, ALU.min),
                                     waits=dw2, inc=(sV, 1))
                        ax = self.op("scalar", lambda eng: eng.activation(
                            self.scr1[:, 514:514 + CH], self.scr1[:, 0:CH],
                            AF.Exp), waits=[(sV, dv)], inc=(sA, 1))
                        mpt = self.op("vector", lambda eng, st=st, Mp_=Mp_:
                                      eng.tensor_tensor(
                                          Mp_[:, st * CH:(st + 1) * CH],
                                          self.scr1[:, 514:514 + CH],
                                          self.Wp[:, st * CH:(st + 1) * CH],
                                          ALU.mult),
                                      waits=[(sA, ax), (sV, wp_done)] +
                                      ([(sT, jfree)] if st == 0 else []))
                    pinfo.append((hh, h, hp, jfree, mpt))
                for (hh, h, hp, jfree, mpt) in pinfo:
                    Mp_, jw_, jD_ = self.Mpb[hp], self.jitwb[hp], self.jitDb[hp]
                    jw0 = [(sT, jfree)]
                    if hb == 0 and hh == 0:
                        jw0.append((sV, max(xt_done, dt_done)))
                        if getattr(self, "_gc_first", True):
                            jw0 += [self.gload_ready, self.consts]
                            self._gc_first = False
                    for st in range(2):
                        self.op("gpsimd_c", lambda eng, h=h, st=st, jw_=jw_:
                                eng.tensor_scalar(
                                    jw_[:, st * 64:st * 64 + 64],
                                    self.xT[:, st * 4096 + h * 64:
                                            st * 4096 + h * 64 + 64],
                                    self.dtT[:, st * 64 + h:st * 64 + h + 1],
                                    None, ALU.mult),
                                waits=jw0 if st == 0 else [])
                    self.op("gpsimd_c", lambda eng, h=h, jD_=jD_:
                            eng.tensor_scalar(
                                jD_[:, :], self.eye_bf[:, :],
                                self.dcol128[:, h:h + 1], None, ALU.mult))
                    jv = self.op("gpsimd_c", lambda eng, h=h, jw_=jw_:
                                 eng.tensor_copy(
                                     jw_[:, 128:192],
                                     self.prev[:, h * 64:h * 64 + 64]),
                                 waits=[(sV, self._prev_rdy)]
                                 if (hh == 0 and hb == 0) else [])
                    rowsl = (hh % 2) * 64
                    colsl = (hh // 2) * 256
                    gD = self.guard(bD) if hh == 0 else []
                    for st in range(2):
                        self.op("tensor", lambda eng, st=st, bD=bD, rowsl=rowsl,
                                colsl=colsl, Mp_=Mp_, jw_=jw_: eng.matmul(
                                    pb[bD][rowsl:rowsl + 64, colsl:colsl + CH],
                                    jw_[:, st * 64:st * 64 + 64],
                                    Mp_[:, st * CH:(st + 1) * CH],
                                    start=(st == 0), stop=False),
                                waits=([(self.sGC, jv), (sV, mpt)] + gD)
                                if st == 0 else [])
                    for st in range(2):
                        self.op("tensor", lambda eng, st=st, bD=bD, rowsl=rowsl,
                                colsl=colsl, h=h, jD_=jD_: eng.matmul(
                                    pb[bD][rowsl:rowsl + 64,
                                           colsl + st * 128:colsl + st * 128 + 128],
                                    self.xT[:, st * 4096 + h * 64:
                                            st * 4096 + h * 64 + 64],
                                    jD_[:, :],
                                    start=False, stop=(st == 1)))
                    gO = self.guard(bO) if hh == 0 else []
                    yoff_t = self.op("tensor",
                            lambda eng, bO=bO, rowsl=rowsl, colsl=colsl, jw_=jw_:
                            eng.matmul(
                                pb[bO][rowsl:rowsl + 64, colsl:colsl + CH],
                                jw_[:, 128:192], self.Cfm[:, 0:CH],
                                start=True, stop=True),
                            waits=gO, inc=(sT, 1))
                    self._jf2[hp] = yoff_t
                bc2p = self.op("tensor", lambda eng, offB=offB, pbank=pbank:
                               eng.matmul(
                                   pb[pbank][:, 0:512], self.ones_f[:, :],
                                   self.rows[0:1, offB:offB + 512],
                                   start=True, stop=True),
                               waits=[(sV, dv)], inc=(sT, 1))
                for j2 in range(2):
                    hh2 = hh0 + j2
                    ub2 = self.convs2 if hh2 // 2 == 0 else self.scr2
                    au_t[hh2] = self.op(
                        "vector", lambda eng, u=ub2, hh2=hh2, j2=j2,
                        pbank=pbank: eng.tensor_copy(
                            u[(hh2 % 2) * 64:(hh2 % 2) * 64 + 64, 0:CH],
                            pb[pbank][(hh2 % 2) * 64:(hh2 % 2) * 64 + 64,
                                      j2 * CH:j2 * CH + CH]),
                        waits=[(sT, bc2p)] if j2 == 0 else [], inc=(sV, 1))
                self.setg(pbank, sV, au_t[hh0 + 1])
                self._mp_war[slot] = bc2p
            ymm_done = self.sT.n
            if getattr(self, "mstop", None) in ("M1", "Y0"):
                fv = self.op("vector", lambda eng, bD=bD, bO=bO: eng.tensor_copy(
                    self.dumc[0:1, 0:1], self.epsc[0:1, 0:1]),
                    waits=[(sT, ymm_done)], inc=(sV, 1))
                self.setg(bD, sV, fv)
                self.setg(bO, sV, fv)
                continue
            for pi in range(2):
                p2 = hb * 2 + pi
                if p2 % 4 == 0:
                    self.zwb, self.zrdy = self.wload(
                        self.d["iprojT"][:, (p2 // 4) * 512:(p2 // 4) * 512 + 512]
                        .rearrange("(ic p) o -> p ic o", p=128))
                bz = 4 + p2 % 2
                gz = self.guard(bz)
                zl = None
                for ic in range(NIC):
                    zl = self.op("tensor", lambda eng, ic=ic, p2=p2, bz=bz,
                                 zwb=self.zwb: eng.matmul(
                                     pb[bz][:, 0:CH],
                                     zwb[:, ic * 512 + (p2 % 4) * 128:
                                         ic * 512 + (p2 % 4) * 128 + 128],
                                     self.xn2[:, ic * 259 + 3:ic * 259 + 259],
                                     start=(ic == 0), stop=(ic == NIC - 1)),
                                 waits=(gz + [self.zrdy]) if ic == 0 else [],
                                 inc=(sT, 1) if ic == NIC - 1 else None)
                if p2 % 4 == 3:
                    self._dummy_wfree(waits=[(sT, zl)])
                zev = self.op("vector", lambda eng, bz=bz: eng.tensor_copy(
                    self.convs[:, 0:CH], pb[bz][:, 0:CH]),
                    waits=[(sT, zl)], inc=(sV, 1))
                self.setg(bz, sV, zev)
                az = self.op("scalar", lambda eng: eng.activation(
                    self.convs[:, 0:CH], self.convs[:, 0:CH], AF.Silu),
                    waits=[(sV, zev)], inc=(sA, 1))
                if self._sq_pend is not None:
                    psq, pslot, pfirst = self._sq_pend
                    self._ssqt[pslot] = self.op(
                        "tensor", lambda eng, pslot=pslot, pfirst=pfirst:
                        eng.matmul(pb[7][0:1, 0:CH], self.onesc_f[:, :],
                                   self.scr2[:, 514 + pslot * CH:
                                             514 + pslot * CH + CH],
                                   start=pfirst, stop=False),
                        waits=[(sA, psq)] + (self.guard(7) if pfirst else []),
                        inc=(sT, 1))
                    self._sq_pend = None
                colsl = pi * 256
                aw_v = max(au_t[2 * pi], au_t[2 * pi + 1])
                for hh2 in range(2):
                    h = p2 * 2 + hh2
                    rowsl = hh2 * 64
                    ub = self.convs2 if pi == 0 else self.scr2
                    self.op("vector", lambda eng, bO=bO, rowsl=rowsl, colsl=colsl,
                            u=ub: eng.tensor_tensor(
                                self.ysc[rowsl:rowsl + 64, 0:CH],
                                pb[bO][rowsl:rowsl + 64, colsl:colsl + CH],
                                u[rowsl:rowsl + 64, 0:CH], ALU.mult),
                            waits=[(sT, ymm_done), (sV, aw_v), (sA, last_sq)]
                            if hh2 == 0 else [])
                    self.op("vector", lambda eng, bD=bD, rowsl=rowsl, colsl=colsl:
                            eng.tensor_tensor(
                                self.ysc[rowsl:rowsl + 64, 0:CH],
                                self.ysc[rowsl:rowsl + 64, 0:CH],
                                pb[bD][rowsl:rowsl + 64, colsl:colsl + CH], ALU.add))
                ygv = self.op("vector", lambda eng: eng.tensor_tensor(
                    self.ysc[:, 0:CH], self.ysc[:, 0:CH], self.convs[:, 0:CH],
                    ALU.mult), waits=[(sA, az)], inc=(sV, 1))
                if pi == 1:
                    self.setg(bD, sV, ygv)
                    self.setg(bO, sV, ygv)
                slot2 = p2 % 2
                sq = self.op("scalar", lambda eng, slot2=slot2: eng.activation(
                    self.scr2[:, 514 + slot2 * CH:514 + slot2 * CH + CH],
                    self.ysc[:, 0:CH], AF.Square),
                    waits=[(sV, ygv), (sT, self._ssqt[slot2])], inc=(sA, 1))
                self._sq_pend = (sq, slot2, ssq_first)
                ssq_first = False
                last_sq = sq
                self.op("vector", lambda eng, p2=p2: eng.tensor_scalar(
                    self.ynfm[:, p2 * CH:(p2 + 1) * CH], self.ysc[:, 0:CH],
                    self.nwattn[:, p2:p2 + 1], None, ALU.mult))
        psq, pslot, pfirst = self._sq_pend
        self._ssqt[pslot] = self.op(
            "tensor", lambda eng, pslot=pslot, pfirst=pfirst:
            eng.matmul(pb[7][0:1, 0:CH], self.onesc_f[:, :],
                       self.scr2[:, 514 + pslot * CH:514 + pslot * CH + CH],
                       start=pfirst, stop=True),
            waits=[(sA, psq)] + (self.guard(7) if pfirst else []),
            inc=(sT, 1))
        self._sq_pend = None
        ssq_mm = self.sT.n
        if getattr(self, "mstop", None) in ("Y", "M1", "Y0"):
            return
        self._scan_states(q, xt_done, dt_done)
        if getattr(self, "mstop", None) == "S":
            return
        R = self.rows
        v1 = self.op("vector", lambda eng: eng.tensor_scalar(
            R[0:1, 1536:1536 + CH], pb[7][0:1, 0:CH], 1.0 / 4096, EPS,
            ALU.mult, ALU.add), waits=[(sT, ssq_mm)], inc=(sV, 1))
        self.setg(7, sV, v1)
        self.op("scalar", lambda eng: eng.activation(
            R[0:1, 1536:1536 + CH], R[0:1, 1536:1536 + CH], AF.Ln),
            waits=[(sV, v1)])
        a1 = self.op("scalar", lambda eng: eng.activation(
            R[0:1, 1536:1536 + CH], R[0:1, 1536:1536 + CH], AF.Exp, scale=-0.5),
            inc=(sA, 1))
        bc = self.op("tensor", lambda eng: eng.matmul(
            pb[6][:, 0:CH], self.ones_f[:, :], R[0:1, 1536:1536 + CH],
            start=True, stop=True), waits=[(sA, a1)] + self.guard(6), inc=(sT, 1))
        rs = self.op("vector", lambda eng: eng.tensor_copy(
            self.ysc[:, 0:CH], pb[6][:, 0:CH]), waits=[(sT, bc)], inc=(sV, 1))
        self.setg(6, sV, rs)
        for dg in range(8):
            wb3, wrdy3 = self.wload(
                self.d["oprojT"][:, dg * 256:(dg + 1) * 256]
                .rearrange("(icc p) o -> p icc o", p=128))
            for dsub in range(2):
                dt_ = dg * 2 + dsub
                bk = 2 + (dt_ % 2)
                wts = list(self.guard(bk))
                if dsub == 0:
                    wts.append(wrdy3)
                if dg == 0 and dsub == 0:
                    wts.append((sV, rs))
                tl = None
                for icc in range(32):
                    tl = self.op("tensor", lambda eng, icc=icc, dsub=dsub, bk=bk,
                                 wb3=wb3: eng.matmul(
                                     pb[bk][:, 0:CH],
                                     wb3[:, icc * 256 + dsub * 128:
                                         icc * 256 + dsub * 128 + 128],
                                     self.ynfm[:, icc * CH:(icc + 1) * CH],
                                     start=(icc == 0), stop=(icc == 31)),
                                 waits=wts if icc == 0 else [],
                                 inc=(sT, 1) if icc == 31 else None)
                self.op("vector", lambda eng, bk=bk: eng.tensor_tensor(
                    self.scr2[:, 0:CH], pb[bk][:, 0:CH], self.ysc[:, 0:CH],
                    ALU.mult), waits=[(sT, tl), (sV, rs)])
                if q < 3:
                    ev = self.op("vector", lambda eng, dt_=dt_, base=base:
                                 eng.tensor_tensor(
                                     self.X[:, dt_ * TH + base + 3:dt_ * TH + base + 256],
                                     self.X[:, dt_ * TH + base + 3:dt_ * TH + base + 256],
                                     self.scr2[:, 0:253], ALU.add))
                    ev = self.op("vector", lambda eng, dt_=dt_, base=base:
                                 eng.tensor_tensor(
                                     self.hsave[:, dt_ * 3:dt_ * 3 + 3],
                                     self.X[:, dt_ * TH + base + 256:dt_ * TH + base + 259],
                                     self.scr2[:, 253:256], ALU.add),
                                 inc=(sV, 1))
                else:
                    ev = self.op("vector", lambda eng, dt_=dt_, base=base:
                                 eng.tensor_tensor(
                                     self.X[:, dt_ * TH + base + 3:dt_ * TH + base + 259],
                                     self.X[:, dt_ * TH + base + 3:dt_ * TH + base + 259],
                                     self.scr2[:, 0:256], ALU.add),
                                 inc=(sV, 1))
                self.setg(bk, sV, ev)
                if dsub == 1:
                    self._dummy_wfree()
        self.resid_done = (sV, self.sV.n)

    def _prev_init(self):
        sT, sV, sA = self.sT, self.sV, self.sA
        pb = self.pb
        d = self.d
        self.op("vector", lambda eng: eng.memset(self.prev[:, :], 0.0),
                waits=[self.consts], inc=(sV, 1))
        last_upd = None
        for j in range(7):
            ld = self.op("sync", lambda eng, j=j: eng.dma_start(
                out=self.pRscr[:, :], in_=d["prevR"][j * 128:(j + 1) * 128, :]),
                inc=(self.sIO, 16),
                waits=([(sV, last_upd)] if last_upd else [self.x_ready]))
            cd = self.op("sync", lambda eng, j=j: eng.dma_start(
                out=self.cd[:, 0:1], in_=d["prevcd"][j * 64:(j + 1) * 64, :]),
                inc=(self.sIO, 16))
            ldv = self.sIO.n
            self.op("vector", lambda eng, j=j: eng.tensor_tensor(
                self.cd[:, 0:1], self.cd[:, 0:1], self.pmaskS[0:64, j:j + 1],
                ALU.mult), waits=[(self.sIO, ldv)])
            v0 = self.op("vector", lambda eng, j=j: eng.tensor_scalar(
                self.smallT[:, 0:1], self.pmaskS[0:64, j:j + 1], -1.0, 1.0,
                ALU.mult, ALU.add), inc=(sV, 1))
            v1 = self.op("vector", lambda eng: eng.tensor_tensor(
                self.cd[:, 0:1], self.cd[:, 0:1], self.smallT[:, 0:1], ALU.add),
                inc=(sV, 1))
            tt = self.op("tensor", lambda eng: eng.transpose(
                pb[5][0:1, 0:64], self.cd[:, 0:1], self.eye_f[0:64, 0:64]),
                waits=[(sV, v1)] + self.guard(5), inc=(sT, 1))
            ve = self.op("vector", lambda eng: eng.tensor_copy(
                self.rows[0:1, 1792:1856], pb[5][0:1, 0:64]),
                waits=[(sT, tt)], inc=(sV, 1))
            tt2 = self.op("tensor", lambda eng: eng.matmul(
                pb[5][:, 0:64], self.ones_f[:, :], self.rows[0:1, 1792:1856],
                start=True, stop=True), waits=[(sV, ve)], inc=(sT, 1))
            ve2 = self.op("vector", lambda eng: eng.tensor_copy(
                self.cdc[:, :], pb[5][:, 0:64]), waits=[(sT, tt2)], inc=(sV, 1))
            self.setg(5, sV, ve2)
            self.op("vector", lambda eng, j=j: eng.tensor_scalar(
                self.pRscr[:, :], self.pRscr[:, :], self.pmaskS[:, j:j + 1],
                None, ALU.mult))
            for h in range(NH):
                self.op("vector", lambda eng, h=h: eng.tensor_scalar(
                    self.prev[:, h * 64:h * 64 + 64],
                    self.prev[:, h * 64:h * 64 + 64],
                    self.cdc[:, h:h + 1], None, ALU.mult))
                self.op("vector", lambda eng, h=h: eng.tensor_tensor(
                    self.prev[:, h * 64:h * 64 + 64],
                    self.prev[:, h * 64:h * 64 + 64],
                    self.pRscr[:, h * 64:h * 64 + 64], ALU.add))
            last_upd = self.op("vector", lambda eng: eng.tensor_copy(
                self.dumc[0:1, 0:1], self.epsc[0:1, 0:1]), inc=(sV, 1))

    def emit_mamba(self, full, first_wait):
        nch = getattr(self, "nchunks", NCHUNK)
        if not full:
            self.op("vector", lambda eng: eng.memset(self.prev[:, :], 0.0),
                    waits=[first_wait], inc=(self.sV, 1))
        for q in range(nch):
            self._chunk2(q, full, first_wait if q == 0 else None,
                         fix_prev=(full and q > 0))
        return (self.sV, self.sV.n)

    def _chunk2(self, q, full, first_wait, fix_prev):
        self._fix_req = fix_prev
        self._chunk(q, full, first_wait)

    def finalize(self):
        with self.nc.Block() as block:
            @block.tensor
            def _(eng):
                for fn in self.q["tensor"]:
                    fn(eng)

            @block.vector
            def _(eng):
                for fn in self.q["vector"]:
                    fn(eng)

            @block.scalar
            def _(eng):
                for fn in self.q["scalar"]:
                    fn(eng)

            @block.gpsimd
            def _(eng):
                for fn in self.q["gpsimd"]:
                    fn(eng)

            @block.sync
            def _(eng):
                for fn in self.q["sync"]:
                    fn(eng)
        return self.nc


def build_l1(debug_stop=None, grps=None, nexp=NEXP, stages="all"):
    p = P("L1", debug_stop, nexp=nexp)
    p.emit_setup()
    ffn_done = p.emit_ffn(grps or FFN1_GRPS, p.x_ready, stages=stages)
    ffn_T = (p.sT, p.sT.n)
    p.op("sync", lambda eng: eng.dma_start(out=p.d["x1out"], in_=p.X[:, :]),
         waits=[ffn_done], inc=(p.sST, 16))
    if debug_stop == "ffn1":
        p.op("sync", lambda eng: eng.dma_start(out=p.d["xnD"], in_=p.xn[:, :]),
             inc=(p.sST, 16))
        p.op("sync", lambda eng: eng.dma_start(out=p.d["gpackD"][0:16, :], in_=p.logit),
             inc=(p.sST, 16))
        p.op("sync", lambda eng: eng.dma_start(out=p.d["gpackD"][16:32, :], in_=p.Gfm_a),
             inc=(p.sST, 16))
        p.op("sync", lambda eng: eng.dma_start(out=p.d["gpackD"][32:48, :], in_=p.Gfm_b),
             inc=(p.sST, 16))
        p.op("sync", lambda eng: eng.dma_start(out=p.d["GbD"], in_=p.Gb[:, :]),
             inc=(p.sST, 16))
        p.op("sync", lambda eng: eng.dma_start(out=p.d["agD"], in_=p.ag[:, :]),
             inc=(p.sST, 16))
    if debug_stop != "ffn1":
        mdone = p.emit_mamba(False, ffn_T)
        p.op("sync", lambda eng: eng.dma_start(out=p.d["Rout"], in_=p.prev[:, :]),
             waits=[mdone], inc=(p.sST, 16))
        p.op("sync", lambda eng: eng.dma_start(out=p.d["cdout"], in_=p.cdp[:, :]),
             waits=[(p.sV, p.sV.n)], inc=(p.sST, 16))
    p.wait("sync", p.sST, p.sST.n)
    return p.finalize()


def build_l2(nexp=NEXP, mstop=None, ffn2=True, states_only=False, nchunks=None, msub=None):
    p = P("L2", nexp=nexp)
    p.mstop = mstop
    p.msub = msub
    if nchunks:
        p.nchunks = nchunks
    p.emit_setup()
    p._prev_init()
    mdone = p.emit_mamba(not states_only, p.x_ready)
    if ffn2 and not states_only and mstop is None:
        ffn_done = p.emit_ffn(FFN2_GRPS, mdone)
    else:
        ffn_done = mdone
    xo = p.d["xout"].rearrange("p (ic t) -> p ic t", ic=NIC)
    xi = p.X[:, :].rearrange("p (ic t) -> p ic t", ic=NIC)[:, :, 3:1027]
    p.op("sync", lambda eng: eng.dma_start(out=xo, in_=xi),
         waits=[ffn_done], inc=(p.sST, 16))
    p.wait("sync", p.sST, p.sST.n)
    return p.finalize()



def host_perm_ab():
    ev = np.arange(0, 4096, 2)
    od = np.arange(1, 4096, 2)
    blocks = []
    for k in range(16):
        blocks.append(ev[k * 128:(k + 1) * 128])
        blocks.append(od[k * 128:(k + 1) * 128])
    return np.concatenate(blocks)


def host_weights(inp):
    d = {}
    perm = host_perm_ab()
    w1 = np.asarray(inp["L0_ffn_w1"])
    d["w1tp"] = np.ascontiguousarray(
        w1.transpose(0, 2, 1)[:, :, perm].reshape(16 * 2048, 4096))
    w2 = np.asarray(inp["L0_ffn_w2"])
    d["w2t"] = np.ascontiguousarray(w2.transpose(0, 2, 1).reshape(16 * 2048, 2048))
    d["iprojT"] = np.ascontiguousarray(np.asarray(inp["L0_attn_in_proj_w"]).T)
    d["oprojT"] = np.ascontiguousarray(np.asarray(inp["L0_attn_out_proj_w"]).T)
    gw = np.asarray(inp["L0_ffn_gate_w"])
    d["gwt"] = np.ascontiguousarray(
        gw.T.reshape(NIC, 128, 16).transpose(1, 0, 2).reshape(128, NIC * 16))
    d["gateb"] = np.ascontiguousarray(np.asarray(inp["L0_ffn_gate_b"]).reshape(1, 16).astype(np.float32))
    b1 = np.asarray(inp["L0_ffn_b1"])[:, perm]
    d["b1sb"] = np.ascontiguousarray(
        b1.reshape(16, 32, 128).transpose(2, 0, 1).reshape(128, 16 * 32))
    d["b2sb"] = np.ascontiguousarray(np.asarray(inp["L0_ffn_b2"]))
    cw = np.asarray(inp["L0_attn_conv_w"])
    d["convw"] = np.ascontiguousarray(
        cw.reshape(34, 128, 4).transpose(1, 0, 2).reshape(128, 34 * 4))
    d["convb"] = np.ascontiguousarray(
        np.asarray(inp["L0_attn_conv_b"]).reshape(34, 128).T)
    d["hsmall"] = np.ascontiguousarray(np.stack(
        [np.asarray(inp["L0_attn_dt_bias"]), np.asarray(inp["L0_attn_A_log"]),
         np.asarray(inp["L0_attn_D"])], axis=1).astype(np.float32))
    d["nwffn"] = np.ascontiguousarray(
        np.asarray(inp["L0_ffn_norm_w"]).reshape(NIC, 128).T)
    anw = np.asarray(inp["L0_attn_norm_weight"])
    d["nwattn"] = np.ascontiguousarray(anw.reshape(32, 128).T)
    D = np.asarray(inp["L0_attn_D"]).astype(np.float32)
    d["dcol128"] = np.ascontiguousarray(np.broadcast_to(D[None, :], (128, 64)))
    m0 = np.zeros((128, CH), np.float32)
    m1 = np.zeros((128, CH), np.float32)
    for s in range(128):
        m0[s, s:] = 1.0
        if 128 + s < CH:
            m1[s, 128 + s:] = 1.0
        m1[s, 128 + s:] = 1.0
    d["mask0"], d["mask1"] = m0, m1
    d["eye_f"] = np.eye(128, dtype=np.float32)
    d["eye_bf"] = np.eye(128, dtype=np.float32).astype(ml_dtypes.bfloat16)
    return d


def host_core_inputs(inp, core):
    u = np.asarray(inp["L0_attn_u"])[0]
    uT = u.T
    t0 = core * TMAIN
    cols = np.zeros((2048, TH), np.float32)
    if core > 0:
        cols[:, 0:3] = uT[:, t0 - 3:t0]
    cols[:, 3:3 + TMAIN] = uT[:, t0:t0 + TMAIN]
    d = {}
    d["uT"] = np.ascontiguousarray(
        cols.reshape(NIC, 128, TH).transpose(1, 0, 2).reshape(128, NIC * TH))
    d["halo_mask"] = np.full((128, 1), 0.0 if core == 0 else 1.0, np.float32)
    return d


def kernel(**inputs):
    wts = host_weights(inputs)
    nc1 = build_l1()
    in_maps1 = []
    for c in range(NCORE):
        m = dict(wts)
        m.pop("oprojT"); m.pop("nwattn"); m.pop("dcol128")
        m.update(host_core_inputs(inputs, c))
        in_maps1.append(m)
    r1 = run_bass_kernel_spmd(nc1, in_maps1, core_ids=list(range(NCORE)))
    nc2 = build_l2()
    in_maps2 = []
    for c in range(NCORE):
        m = dict(wts)
        m["x1in"] = r1.results[c]["x1out"]
        pR = np.zeros((7 * 128, DIN), np.float32)
        pcd = np.zeros((7 * 64, 1), np.float32)
        pm = np.zeros((128, 7), np.float32)
        for j in range(7):
            if j < c:
                pR[j * 128:(j + 1) * 128] = r1.results[j]["Rout"]
                pcd[j * 64:(j + 1) * 64] = r1.results[j]["cdout"]
                pm[:, j] = 1.0
        m["prevR"], m["prevcd"], m["pmask"] = pR, pcd, pm
        m["halo_mask"] = in_maps1[c]["halo_mask"]
        in_maps2.append(m)
    r2 = run_bass_kernel_spmd(nc2, in_maps2, core_ids=list(range(NCORE)))
    outs = []
    for c in range(NCORE):
        o = r2.results[c]["xout"]
        outs.append(o.reshape(128, NIC, TMAIN).transpose(1, 0, 2).reshape(2048, TMAIN))
    full = np.concatenate(outs, axis=1)
    return np.ascontiguousarray(full.T)[None]
